# version 15
# baseline (speedup 1.0000x reference)
import numpy as np
import ml_dtypes
import concourse.bass as bass
import concourse.mybir as mybir
from concourse.bass_utils import run_bass_kernel_spmd
from contextlib import ExitStack

F32 = mybir.dt.float32
BF16 = mybir.dt.bfloat16
I32 = mybir.dt.int32
AF = mybir.ActivationFunctionType
ALU = mybir.AluOpType
bfdt = ml_dtypes.bfloat16

S = 2048
E = 2048
EC = 16
CQ = 1536
CQC = 12
CKV = 512
CKVC = 4
RD = 64
H = 16
HD = 128
HID = 5632
HIDC = 44
NSEQ = 2
TT = 1024
NSUB = 2
NTILE = S // TT
NG = 512
SCALE = float(1.0 / np.sqrt(HD + RD))
ESHIFT = -12.0
EPS = 1e-5
QG = 512
NQG = S // QG
KCPG = QG // 128


class Buf:

    def __init__(self):
        self.reads = []
        self.wr = None

    def for_write(self):
        w = list(self.reads)
        if self.wr is not None:
            w.append(self.wr)
        return w

    def for_read(self):
        return [self.wr] if self.wr is not None else []

    def did_write(self, sem, val):
        self.reads = []
        self.wr = (sem, val)

    def did_read(self, sem, val):
        self.reads.append((sem, val))


class Ctx:

    ENGS = ("pe", "act", "dve", "sp", "pl")
    DMA_SEMS = ("sp", "pl", "st0", "st1", "lnb", "g0", "g1")
    ALL_SEMS = ENGS + ("st0", "st1", "lnb", "g0", "g1")

    def __init__(self, nc, es):
        self.nc = nc
        self.sem = {n: es.enter_context(nc.semaphore("sem_" + n))
                    for n in self.ALL_SEMS}
        self.cnt = {n: 0 for n in self.ALL_SEMS}
        self.lastw = {}
        self.q = {n: [] for n in self.ENGS}

    def step(self, ename, emit, reads=(), writes=(), inc=1, dsem=None):
        sname = dsem or ename
        waits = []
        for b in reads:
            waits += b.for_read()
        for b in writes:
            waits += b.for_write()
        wlist = []
        need = {}
        for dep, val in waits:
            if dep == sname:
                continue
            assert val <= self.cnt[dep], (ename, dep, val, self.cnt[dep])
            if dep in self.DMA_SEMS:
                val = self.cnt[dep]
            need[dep] = max(need.get(dep, 0), val)
        for dep, val in need.items():
            if self.lastw.get((ename, dep), -1) < val:
                self.lastw[(ename, dep)] = val
                wlist.append((dep, val))
        n = 16 if sname in self.DMA_SEMS else 1
        self.cnt[sname] += n * inc
        mark = self.cnt[sname]
        sem = self.sem[sname]

        def run(eng, wlist=wlist, emit=emit, sem=sem, n=n, inc=inc):
            for dep, val in wlist:
                eng.wait_ge(self.sem[dep], val)
            last = emit(eng)
            if inc:
                assert last is not None
                if isinstance(last, (list, tuple)):
                    assert len(last) == inc
                    for one in last:
                        one.then_inc(sem, n)
                else:
                    last.then_inc(sem, n * inc)

        self.q[ename].append(run)
        for b in reads:
            b.did_read(sname, mark)
        for b in writes:
            b.did_write(sname, mark)
        return mark

    def raw_wait(self, ename, dep=None, val=None):
        dep = dep or ename
        val = self.cnt[dep] if val is None else val

        def run(eng, dep=dep, val=val):
            eng.wait_ge(self.sem[dep], val)
        self.q[ename].append(run)
        self.lastw[(ename, dep)] = max(self.lastw.get((ename, dep), -1), val)

    SOFT = False

    def soft_barrier(self):
        if not self.SOFT:
            self.flush()
            return
        for e in self.ENGS:
            for d in self.ALL_SEMS:
                if d == e:
                    continue
                v = self.cnt[d]
                if v > 0 and self.lastw.get((e, d), -1) < v:
                    self.raw_wait(e, d, v)

    def flush(self):
        qs = self.q
        self.q = {n: [] for n in self.ENGS}
        if not any(qs.values()):
            return
        for dma in self.DMA_SEMS:
            v = self.cnt[dma]
            if v > 0:
                tgt = "pl" if qs["pl"] else "sp"

                def fin(eng, dma=dma, v=v):
                    eng.wait_ge(self.sem[dma], v)
                qs[tgt].append(fin)

        def mk(name):
            def body(eng):
                for fn in qs[name]:
                    fn(eng)
            return body

        with self.nc.Block() as blk:
            if qs["pe"]:
                blk.tensor(mk("pe"))
            if qs["act"]:
                blk.scalar(mk("act"))
            if qs["dve"]:
                blk.vector(mk("dve"))
            if qs["sp"]:
                blk.sync(mk("sp"))
            if qs["pl"]:
                blk.gpsimd(mk("pl"))
        for e in self.ENGS:
            for nm in self.ALL_SEMS:
                self.lastw[(e, nm)] = max(self.lastw.get((e, nm), -1), self.cnt[nm])



def build_core_program(dumps=(), stop_after=None, nseq=NSEQ):
    nc = bass.Bass("TRN2", target_bir_lowering=False)

    x_in = nc.dram_tensor("x_in", [nseq, E, S], F32, kind="ExternalInput").ap()
    wdqT = nc.dram_tensor("wdqT", [E, CQ], F32, kind="ExternalInput").ap()
    wuqT = nc.dram_tensor("wuqT", [CQ, E], F32, kind="ExternalInput").ap()
    wdkvT = nc.dram_tensor("wdkvT", [E, CKV], F32, kind="ExternalInput").ap()
    wukT = nc.dram_tensor("wukT", [CKV, E], F32, kind="ExternalInput").ap()
    wuvT = nc.dram_tensor("wuvT", [CKV, E], F32, kind="ExternalInput").ap()
    wqrT = nc.dram_tensor("wqrT", [CQ, H * RD], F32, kind="ExternalInput").ap()
    wkrT = nc.dram_tensor("wkrT", [E, H * RD], F32, kind="ExternalInput").ap()
    woutT = nc.dram_tensor("woutT", [E, E], F32, kind="ExternalInput").ap()
    wgT = nc.dram_tensor("wgT", [E, HID], I32, kind="ExternalInput").ap()
    wuT = nc.dram_tensor("wuT", [E, HID], I32, kind="ExternalInput").ap()
    wdT = nc.dram_tensor("wdT", [HID, E], I32, kind="ExternalInput").ap()
    sg_in = nc.dram_tensor("sg_in", [128, HIDC], F32, kind="ExternalInput").ap()
    su_in = nc.dram_tensor("su_in", [128, HIDC], F32, kind="ExternalInput").ap()
    sd_in = nc.dram_tensor("sd_in", [128, EC], F32, kind="ExternalInput").ap()
    cos_in = nc.dram_tensor("cos_in", [128, S], BF16, kind="ExternalInput").ap()
    sin_in = nc.dram_tensor("sin_in", [128, S], BF16, kind="ExternalInput").ap()
    mask_in = nc.dram_tensor("mask_in", [4, 128, NG], BF16, kind="ExternalInput").ap()
    ones_c = nc.dram_tensor("ones_c", [128, 1], BF16, kind="ExternalInput").ap()
    ones_r = nc.dram_tensor("ones_r", [1, 128], BF16, kind="ExternalInput").ap()

    y_out = nc.dram_tensor("y_out", [nseq, E, S], F32, kind="ExternalOutput").ap()

    wg_s = nc.dram_tensor("wg_s", [HIDC, 128, EC * 128], BF16, kind="Internal").ap()
    wu_s = nc.dram_tensor("wu_s", [HIDC, 128, EC * 128], BF16, kind="Internal").ap()
    wd_s = nc.dram_tensor("wd_s", [EC, 128, HIDC * 128], BF16, kind="Internal").ap()
    x1_s = nc.dram_tensor("x1_s", [E, S], F32, kind="Internal").ap()
    x2_s = nc.dram_tensor("x2_s", [E, S], F32, kind="Internal").ap()
    ao_s = nc.dram_tensor("ao_s", [H, HD, S], BF16, kind="Internal").ap()

    dump_aps = {}
    for name in dumps:
        src = {"x1": x1_s, "x2": x2_s, "ao": ao_s}[name]
        dump_aps[name] = (nc.dram_tensor("dump_" + name, list(src.shape),
                                         src.dtype, kind="ExternalOutput").ap(), src)

    es = ExitStack()
    with es:
        C = Ctx(nc, es)

        def sb(name, shape, dtype, stack=es):
            return stack.enter_context(nc.sbuf_tensor(name, shape, dtype))

        def pst(name, shape, dtype=F32, stack=es):
            return stack.enter_context(nc.psum_tensor(name, shape, dtype))

        g = dict(
            cosT=sb("cosT", [128, S], BF16), sinT=sb("sinT", [128, S], BF16),
            sg=sb("sg", [128, HIDC], F32), su=sb("su", [128, HIDC], F32),
            sdh=sb("sdh", [128, EC], F32),
            onec=sb("onec", [128, 1], BF16), oner=sb("oner", [1, 128], BF16),
            c_eps=sb("c_eps", [1, 1], F32), c_shift=sb("c_shift", [128, 1], F32),
        )
        lnA = sb("lnA", [128, EC * TT], BF16)
        lnA_b = Buf()
        g["mask_in"] = mask_in

        consts = Buf()
        for dst, src in ((g["cosT"], cos_in), (g["sinT"], sin_in),
                         (g["sg"], sg_in), (g["su"], su_in), (g["sdh"], sd_in),
                         (g["onec"], ones_c), (g["oner"], ones_r)):
            C.step("sp", lambda e, dst=dst, src=src: e.dma_start(out=dst[:], in_=src),
                   writes=[consts])
        C.step("dve", lambda e: e.tensor_scalar_mul(g["sdh"][:], g["sdh"][:], 0.5),
               reads=[consts], writes=[consts])
        C.step("dve", lambda e: e.memset(g["c_eps"][:], EPS), writes=[consts])
        C.step("dve", lambda e: e.memset(g["c_shift"][:], ESHIFT), writes=[consts])
        C.flush()

        with ExitStack() as st0:
            p0i = sb("p0_i", [128, 4 * 2048], I32, st0)
            p0b = sb("p0_b", [128, 4 * 2048], BF16, st0)
            p0is = [Buf() for _ in range(4)]
            p0bs = [Buf() for _ in range(4)]
            jobs0 = ([("wg", wgT, wg_s, oc, EC) for oc in range(HIDC)] +
                     [("wu", wuT, wu_s, oc, EC) for oc in range(HIDC)] +
                     [("wd", wdT, wd_s, oc, HIDC) for oc in range(EC)])
            jctr = 0
            for kind0, srcw0, dstw0, oc0, nic0 in jobs0:
                for c00 in range(0, nic0, 16):
                    n0 = min(16, nic0 - c00)
                    sl0 = jctr % 4
                    q0 = ("sp", "pl")[jctr % 2]
                    jctr += 1
                    X0 = n0 * 128
                    C.step(q0, lambda e, srcw0=srcw0, oc0=oc0, sl0=sl0, X0=X0, n0=n0, c00=c00:
                           e.dma_start(out=p0i.ap()[:, sl0 * 2048:sl0 * 2048 + X0].rearrange(
                                           "p (a n) -> p a n", a=n0),
                                       in_=srcw0.rearrange("(a p) m -> p a m", p=128)[
                                           :, c00:c00 + n0, oc0 * 128:(oc0 + 1) * 128]),
                           writes=[p0is[sl0]], dsem="g%d" % (sl0 % 2))
                    C.step("dve", lambda e, sl0=sl0, X0=X0:
                           e.tensor_scalar_add(p0b[:, sl0 * 2048:sl0 * 2048 + X0],
                                               p0i[:, sl0 * 2048:sl0 * 2048 + X0], -128.0),
                           reads=[p0is[sl0]], writes=[p0bs[sl0]])
                    C.step(("pl", "sp")[jctr % 2],
                           lambda e, dstw0=dstw0, oc0=oc0, sl0=sl0, X0=X0, c00=c00:
                           e.dma_start(out=dstw0[oc0][:, c00 * 128:c00 * 128 + X0],
                                       in_=p0b[:, sl0 * 2048:sl0 * 2048 + X0]),
                           reads=[p0bs[sl0]], dsem="st%d" % (sl0 % 2))
            C.soft_barrier()

        x1_db = [Buf() for _ in range(NTILE)]
        x2_db = [Buf() for _ in range(NTILE)]
        _ln_load(nc, C, lnA, lnA_b, x_in[0], 0)
        for s in range(nseq):
            _ffn_pass(nc, C, sb, pst, x_in[s], x1_s, wg_s, wu_s, wd_s, g,
                      "f1s%d" % s, lnA, lnA_b, x1_db, dq=None)
            _ln_load(nc, C, lnA, lnA_b, x1_s, 0, src_b=x1_db[0])
            C.soft_barrier()
            if stop_after == "P1" and s == nseq - 1:
                _finish(nc, C, dump_aps)
                return nc

            with ExitStack() as st23:
                c_q = sb("cq%d" % s, [128, CQC * S], BF16, st23)
                c_kv = sb("ckv%d" % s, [128, CKVC * S], BF16, st23)
                k_r = sb("kr%d" % s, [128, (H // 2) * S], BF16, st23)
                lat = Buf()
                _proj_pass(nc, C, sb, pst, x1_s, c_q, c_kv, k_r, lat,
                           wdqT, wdkvT, wkrT, g, "p2s%d" % s, lnA, lnA_b, x1_db)
                C.soft_barrier()
                if stop_after == "P2" and s == nseq - 1:
                    _finish(nc, C, dump_aps,
                            extra=[("c_q", c_q), ("c_kv", c_kv), ("k_r", k_r)])
                    return nc
                _attention(nc, C, sb, pst, c_q, c_kv, k_r, lat,
                           wuqT, wqrT, wukT, wuvT, g, ao_s, "p3s%d" % s)
            if stop_after == "P3" and s == nseq - 1:
                _finish(nc, C, dump_aps)
                return nc

            _outproj(nc, C, sb, pst, ao_s, woutT, x1_s, x2_s, g, "oas%d" % s,
                     x1_db, x2_db)
            _ln_load(nc, C, lnA, lnA_b, x2_s, 0, src_b=x2_db[0])
            C.soft_barrier()
            if stop_after == "P4a" and s == nseq - 1:
                _finish(nc, C, dump_aps)
                return nc

            _ffn_pass(nc, C, sb, pst, x2_s, y_out[s], wg_s, wu_s, wd_s, g,
                      "f2s%d" % s, lnA, lnA_b, None)
            if s + 1 < nseq:
                _ln_load(nc, C, lnA, lnA_b, x_in[s + 1], 0)
            C.soft_barrier()

        _finish(nc, C, dump_aps)
    return nc


def _finish(nc, C, dump_aps, extra=None):
    for name, (dap, src) in dump_aps.items():
        C.step("pl", lambda e, dap=dap, src=src: e.dma_start(out=dap, in_=src))
    if extra:
        for nm, t in extra:
            dap = nc.dram_tensor("dumpsb_" + nm, list(t.ap().shape),
                                 t.ap().dtype, kind="ExternalOutput").ap()
            C.step("pl", lambda e, dap=dap, t=t: e.dma_start(out=dap, in_=t[:]))
    C.raw_wait("pl")
    C.raw_wait("sp")
    C.flush()


def _ln_load(nc, C, lnb, lnb_b, src_view, t0, src_b=None):
    for sub in range(NSUB):
        sl = lnb.ap().rearrange("p (c t) -> p c t", c=EC)[:, :, sub * NG:(sub + 1) * NG]
        C.step("pl", lambda e, sl=sl, sub=sub: e.dma_start(
            out=sl,
            in_=src_view.rearrange("(c p) t -> p c t", p=128)[
                :, :, t0 + sub * NG:t0 + (sub + 1) * NG]),
            reads=([src_b] if src_b is not None else []), writes=[lnb_b],
            dsem="lnb")


def _ln_compute(nc, C, lnb, lnb_b, sqb, sqb_b, tmpb, tmpb_b, rowset, sps, spb, g):
    rb, rowbufs = rowset
    for sub in range(NSUB):
        s0 = sub * NG
        for c in range(EC):
            csl = lnb[:, c * TT + s0:c * TT + s0 + NG]
            ssl = sqb[:, (c % 2) * NG:(c % 2 + 1) * NG]
            C.step("dve", lambda e, csl=csl, ssl=ssl: e.tensor_mul(ssl, csl, csl),
                   reads=[lnb_b], writes=[sqb_b[c % 2]])

            def mm(e, c=c, csl=csl, ssl=ssl):
                e.matmul(sps[0:1, 0:NG], g["onec"][:], csl,
                         start=(c == 0), stop=(c == EC - 1))
                return e.matmul(sps[0:1, NG:2 * NG], g["onec"][:], ssl,
                                start=(c == 0), stop=(c == EC - 1))
            C.step("pe", mm, reads=[lnb_b, sqb_b[c % 2], consts_of(g)],
                   writes=[spb[0], spb[1]])
        n = 1.0 / E

        def rows(e, sub=sub):
            e.tensor_scalar_mul(rb["mean"][0:1, sub * NG:(sub + 1) * NG],
                                sps[0:1, 0:NG], n)
            e.tensor_scalar_mul(rb["ex2"][0:1, sub * NG:(sub + 1) * NG],
                                sps[0:1, NG:2 * NG], n)
            e.tensor_mul(rb["msq"][0:1, sub * NG:(sub + 1) * NG],
                         rb["mean"][0:1, sub * NG:(sub + 1) * NG],
                         rb["mean"][0:1, sub * NG:(sub + 1) * NG])
            return e.tensor_sub(rb["var"][0:1, sub * NG:(sub + 1) * NG],
                                rb["ex2"][0:1, sub * NG:(sub + 1) * NG],
                                rb["msq"][0:1, sub * NG:(sub + 1) * NG])
        C.step("dve", rows, reads=[spb[0], spb[1]], writes=[rowbufs])

    def rstd(e):
        e.activation(rb["lnv"][:], rb["var"][:], AF.Ln, bias=g["c_eps"][:])
        return e.activation(rb["rstd"][:], rb["lnv"][:], AF.Exp, scale=-0.5)
    C.step("act", rstd, reads=[rowbufs, consts_of(g)], writes=[rowbufs])

    def tobf(e):
        e.tensor_copy(rb["mean_b"][:], rb["mean"][:])
        return e.tensor_copy(rb["rstd_b"][:], rb["rstd"][:])
    C.step("dve", tobf, reads=[rowbufs], writes=[rowbufs])

    for sub in range(NSUB):
        s0 = sub * NG

        def bcast(e, sub=sub):
            e.matmul(sps[:, 0:NG], g["oner"][:],
                     rb["mean_b"][0:1, sub * NG:(sub + 1) * NG],
                     start=True, stop=True)
            return e.matmul(sps[:, NG:2 * NG], g["oner"][:],
                            rb["rstd_b"][0:1, sub * NG:(sub + 1) * NG],
                            start=True, stop=True)
        C.step("pe", bcast, reads=[rowbufs, consts_of(g)], writes=[spb[0], spb[1]])

        for c in range(EC):
            csl = lnb[:, c * TT + s0:c * TT + s0 + NG]
            tsl = tmpb[:, (c % 2) * NG:(c % 2 + 1) * NG]

            def norm(e, csl=csl, tsl=tsl):
                e.tensor_tensor(tsl, csl, sps[:, 0:NG], ALU.subtract)
                return e.tensor_tensor(csl, tsl, sps[:, NG:2 * NG], ALU.mult)
            C.step("dve", norm, reads=[spb[0], spb[1], lnb_b],
                   writes=[lnb_b, tmpb_b[c % 2]])


_CONSTS_BUF = {}


def consts_of(g):
    key = id(g)
    if key not in _CONSTS_BUF:
        _CONSTS_BUF[key] = Buf()
    return _CONSTS_BUF[key]


def _mk_rowset(sb, stack, tag):
    W = NSUB * NG
    R = sb(tag + "_rows", [1, 3 * W], F32, stack)
    B = sb(tag + "_rowsb", [1, 2 * W], BF16, stack)
    rb = dict(
        mean=R[0:1, 0:W], ex2=R[0:1, W:2 * W], msq=R[0:1, 2 * W:3 * W],
        var=R[0:1, W:2 * W], lnv=R[0:1, 2 * W:3 * W], rstd=R[0:1, W:2 * W],
        mean_b=B[0:1, 0:W], rstd_b=B[0:1, W:2 * W],
    )
    return rb, Buf()


def _ffn_pass(nc, C, sb, pst, src_dram, dst_dram, wg_s, wu_s, wd_s, g, tag,
              lnA, lnA_b, dst_db, dq=None):
    with ExitStack() as stF:
        lnb, lnb_b = lnA, lnA_b
        hbuf = sb(tag + "_h", [128, HIDC * TT], BF16, stF)
        wgbuf = sb(tag + "_wg", [128, 2 * 2048], BF16, stF)
        wubuf = sb(tag + "_wu", [128, 2 * 2048], BF16, stF)
        wdbuf = sb(tag + "_wd", [128, 2 * HID], BF16, stF)
        gbuf = sb(tag + "_g", [128, 2 * NG], BF16, stF)
        sqb = sb(tag + "_sq", [128, 2 * NG], BF16, stF)
        tmpb = sb(tag + "_tmp", [128, 2 * NG], BF16, stF)
        xep = sb(tag + "_xep", [128, 2 * NG], F32, stF)
        x1o = sb(tag + "_x1o", [128, 2 * NG], F32, stF)
        rowset = _mk_rowset(sb, stF, tag)
        if dq:
            wgT_, wuT_, wdT_ = dq
            ibuf = sb(tag + "_ib", [128, 2 * 1024], I32, stF)
            isl = [Buf(), Buf()]
            wdst = sb(tag + "_wds", [128, 1024], BF16, stF)
            wdsl_st = [Buf()]

        h_b = [Buf() for _ in range(HIDC * 2)]
        wgsl = [Buf(), Buf()]
        wusl = [Buf(), Buf()]
        wdsl = [Buf(), Buf()]
        gsl = [Buf(), Buf()]
        sqsl = [Buf(), Buf()]
        tmsl = [Buf(), Buf()]
        xsl = [Buf(), Buf()]
        osl = [Buf(), Buf()]

        gate_ps = pst(tag + "_gp", [128, TT], F32, stF)
        up_ps = pst(tag + "_up", [128, TT], F32, stF)
        down_ps = pst(tag + "_dp", [128, TT], F32, stF)
        gate_pb = [Buf(), Buf()]
        up_pb = [Buf(), Buf()]
        down_pb = [Buf(), Buf()]

        _ln_compute(nc, C, lnb, lnb_b, sqb, sqsl, tmpb, tmsl, rowset,
                    gate_ps, gate_pb, g)

        wdram = {}
        wd_done = [0]
        dqctr = [0]

        def dq_chain(kind, srcw, scratch, oc, wbuf_, wsl_, sl, nic, c0=0):
            X = nic * 128
            W = 2048 if kind != "wd" else HID
            il = dqctr[0] % 2
            dqctr[0] += 1
            C.step("sp", lambda e: e.dma_start(
                out=ibuf.ap()[:, il * 1024:il * 1024 + X].rearrange(
                    "p (a n) -> p a n", a=nic),
                in_=srcw.rearrange("(a p) m -> p a m", p=128)[
                    :, c0:c0 + nic, oc * 128:(oc + 1) * 128]),
                writes=[isl[il]], dsem="g%d" % il)
            if wbuf_ is None:
                st = 0

                C.step("dve", lambda e: e.tensor_scalar_add(
                    wdst[:, 0:X],
                    ibuf[:, il * 1024:il * 1024 + X], -128.0),
                    reads=[isl[il]], writes=[wdsl_st[0]])
                db = wdram.setdefault((kind, oc), Buf())
                C.step("pl", lambda e: e.dma_start(
                    out=scratch[oc][:, c0 * 128:c0 * 128 + X],
                    in_=wdst[:, 0:X]),
                    reads=[wdsl_st[0]], writes=[db], dsem="st0")
                return
            C.step("dve", lambda e: e.tensor_scalar_add(
                wbuf_[:, sl * W + c0 * 128:sl * W + c0 * 128 + X],
                ibuf[:, il * 1024:il * 1024 + X], -128.0),
                reads=[isl[il]], writes=[wsl_])
            db = wdram.setdefault((kind, oc), Buf())
            C.step("pl", lambda e: e.dma_start(
                out=scratch[oc][:, c0 * 128:c0 * 128 + X],
                in_=wbuf_[:, sl * W + c0 * 128:sl * W + c0 * 128 + X]),
                reads=[wsl_], writes=[db], dsem="st%d" % sl)

        for tile in range(NTILE):
            t0 = tile * TT
            for oc in range(HIDC):
                sl = oc % 2

                if dq and tile == 0:
                    dq_chain("wg", wgT_, wg_s, oc, wgbuf, wgsl[sl], sl, 8, 0)
                    dq_chain("wg", wgT_, wg_s, oc, wgbuf, wgsl[sl], sl, 8, 8)
                    dq_chain("wu", wuT_, wu_s, oc, wubuf, wusl[sl], sl, 8, 0)
                    dq_chain("wu", wuT_, wu_s, oc, wubuf, wusl[sl], sl, 8, 8)
                    wd_subs = [(og, c0, nic) for og in range(EC)
                               for c0, nic in ((0, 8), (8, 8), (16, 8),
                                               (24, 8), (32, 8), (40, 4))]
                    while wd_done[0] < (oc + 1) * len(wd_subs) // HIDC:
                        og, c0, nic = wd_subs[wd_done[0]]
                        dq_chain("wd", wdT_, wd_s, og, None, None, 0, nic, c0)
                        wd_done[0] += 1
                else:
                    def wload(e, oc=oc, sl=sl):
                        a = e.dma_start(out=wgbuf[:, sl * 2048:(sl + 1) * 2048],
                                        in_=wg_s[oc])
                        b = e.dma_start(out=wubuf[:, sl * 2048:(sl + 1) * 2048],
                                        in_=wu_s[oc])
                        return [a, b]
                    rds = [wdram[k] for k in (("wg", oc), ("wu", oc)) if k in wdram]
                    C.step("sp", wload, reads=rds,
                           writes=[wgsl[sl], wusl[sl]], inc=2)

                for ng in range(2):
                    def gmm(e, oc=oc, sl=sl, ng=ng):
                        for ic in range(EC):
                            ins = e.matmul(
                                gate_ps[:, ng * NG:(ng + 1) * NG],
                                wgbuf[:, sl * 2048 + ic * 128:sl * 2048 + (ic + 1) * 128],
                                lnb[:, ic * TT + ng * NG:ic * TT + (ng + 1) * NG],
                                start=(ic == 0), stop=(ic == EC - 1))
                        return ins
                    C.step("pe", gmm, reads=[wgsl[sl], lnb_b], writes=[gate_pb[ng]])

                    def umm(e, oc=oc, sl=sl, ng=ng):
                        for ic in range(EC):
                            ins = e.matmul(
                                up_ps[:, ng * NG:(ng + 1) * NG],
                                wubuf[:, sl * 2048 + ic * 128:sl * 2048 + (ic + 1) * 128],
                                lnb[:, ic * TT + ng * NG:ic * TT + (ng + 1) * NG],
                                start=(ic == 0), stop=(ic == EC - 1))
                        return ins
                    C.step("pe", umm, reads=[wusl[sl], lnb_b], writes=[up_pb[ng]])

                    C.step("act", lambda e, oc=oc, ng=ng: e.activation(
                        gbuf[:, ng * NG:(ng + 1) * NG],
                        gate_ps[:, ng * NG:(ng + 1) * NG],
                        AF.Silu, scale=g["sg"][:, oc:oc + 1]),
                        reads=[gate_pb[ng], consts_of(g)], writes=[gsl[ng]])

                    C.step("dve", lambda e, oc=oc, ng=ng: e.scalar_tensor_tensor(
                        hbuf[:, oc * TT + ng * NG:oc * TT + (ng + 1) * NG],
                        up_ps[:, ng * NG:(ng + 1) * NG],
                        g["su"][:, oc:oc + 1],
                        gbuf[:, ng * NG:(ng + 1) * NG],
                        ALU.mult, ALU.mult),
                        reads=[up_pb[ng], gsl[ng], consts_of(g)],
                        writes=[h_b[oc * 2 + ng]])

            if tile + 1 < NTILE:
                _ln_load(nc, C, lnb, lnb_b, src_dram, t0 + TT)

            for og in range(EC):
                sl = og % 2
                rds = [wdram[k] for k in (("wd", og),) if k in wdram]
                C.step("sp", lambda e, og=og, sl=sl: e.dma_start(
                    out=wdbuf[:, sl * HID:(sl + 1) * HID], in_=wd_s[og]),
                    reads=rds, writes=[wdsl[sl]])
                for ng in range(2):
                    xslot = ng
                    C.step("sp", lambda e, og=og, ng=ng, xslot=xslot, t0=t0:
                           e.dma_start(
                               out=xep[:, xslot * NG:(xslot + 1) * NG],
                               in_=src_dram[og * 128:(og + 1) * 128,
                                            t0 + ng * NG:t0 + (ng + 1) * NG]),
                           writes=[xsl[xslot]])

                    def dmms(e, og=og, sl=sl, ng=ng):
                        for hc in range(HIDC):
                            ins = e.matmul(
                                down_ps[:, ng * NG:(ng + 1) * NG],
                                wdbuf[:, sl * HID + hc * 128:sl * HID + (hc + 1) * 128],
                                hbuf[:, hc * TT + ng * NG:hc * TT + (ng + 1) * NG],
                                start=(hc == 0), stop=(hc == HIDC - 1))
                        return ins
                    C.step("pe", dmms,
                           reads=[wdsl[sl]] + [h_b[hc * 2 + ng] for hc in range(HIDC)],
                           writes=[down_pb[ng]])

                    C.step("dve", lambda e, og=og, ng=ng, xslot=xslot:
                           e.scalar_tensor_tensor(
                               x1o[:, xslot * NG:(xslot + 1) * NG],
                               down_ps[:, ng * NG:(ng + 1) * NG],
                               g["sdh"][:, og:og + 1],
                               xep[:, xslot * NG:(xslot + 1) * NG],
                               ALU.mult, ALU.add),
                           reads=[down_pb[ng], xsl[xslot], consts_of(g)],
                           writes=[osl[xslot]])

                    C.step("pl", lambda e, og=og, ng=ng, xslot=xslot, t0=t0:
                           e.dma_start(
                               out=dst_dram[og * 128:(og + 1) * 128,
                                            t0 + ng * NG:t0 + (ng + 1) * NG],
                               in_=x1o[:, xslot * NG:(xslot + 1) * NG]),
                           reads=[osl[xslot]],
                           writes=([dst_db[tile]] if dst_db else []),
                           dsem="st%d" % ng)
                if og == 7 and tile + 1 < NTILE:
                    _ln_compute(nc, C, lnb, lnb_b, sqb, sqsl, tmpb, tmsl,
                                rowset, gate_ps, gate_pb, g)


def _proj_pass(nc, C, sb, pst, x1_s, c_q, c_kv, k_r, lat,
               wdqT, wdkvT, wkrT, g, tag, lnA, lnA_b, x1_db):
    with ExitStack() as stP:
        lnbs = [lnA, sb(tag + "_ln1", [128, EC * TT], BF16, stP)]
        wbuf = sb(tag + "_w", [128, 2 * 2048], BF16, stP)
        sqb = sb(tag + "_sq", [128, 2 * NG], BF16, stP)
        tmpb = sb(tag + "_tmp", [128, 2 * NG], BF16, stP)
        rop1 = sb(tag + "_r1", [128, TT], F32, stP)
        rop2 = sb(tag + "_r2", [128, TT], F32, stP)
        rowset = _mk_rowset(sb, stP, tag)
        lnb_bs = [lnA_b, Buf()]
        wsl = [Buf(), Buf()]
        sqsl = [Buf(), Buf()]
        tmsl = [Buf(), Buf()]
        ropb = Buf()

        pj_ps = pst(tag + "_pj", [128, 2 * NG], F32, stP)
        kr_ps = pst(tag + "_kr", [128, TT], F32, stP)
        pj_pb = [Buf(), Buf()]
        kr_pb = Buf()

        _ln_compute(nc, C, lnbs[0], lnb_bs[0], sqb, sqsl, tmpb, tmsl, rowset,
                    pj_ps, pj_pb, g)

        for tile in range(NTILE):
            t0 = tile * TT
            lnb = lnbs[tile % 2]
            lnb_b = lnb_bs[tile % 2]
            if tile + 1 < NTILE:
                _ln_load(nc, C, lnbs[(tile + 1) % 2], lnb_bs[(tile + 1) % 2],
                         x1_s, t0 + TT, src_b=x1_db[tile + 1])
            jobs = ([("cq", wdqT, oc, c_q) for oc in range(CQC)] +
                    [("ckv", wdkvT, oc, c_kv) for oc in range(CKVC)] +
                    [("kr", wkrT, oc, k_r) for oc in range(H // 2)])
            for j, (kind, wsrc, oc, dbuf) in enumerate(jobs):
                sl = j % 2
                C.step("pl", lambda e, wsrc=wsrc, oc=oc, sl=sl: e.dma_start(
                    out=wbuf.ap()[:, sl * 2048:(sl + 1) * 2048].rearrange(
                        "p (a n) -> p a n", a=EC),
                    in_=wsrc.rearrange("(a p) m -> p a m", p=128)[
                        :, :, oc * 128:(oc + 1) * 128]),
                    writes=[wsl[sl]], dsem="g%d" % sl)

                if kind != "kr":
                    for ng in range(2):
                        def pmm(e, sl=sl, ng=ng, lnb=lnb):
                            for ic in range(EC):
                                ins = e.matmul(
                                    pj_ps[:, ng * NG:(ng + 1) * NG],
                                    wbuf[:, sl * 2048 + ic * 128:sl * 2048 + (ic + 1) * 128],
                                    lnb[:, ic * TT + ng * NG:ic * TT + (ng + 1) * NG],
                                    start=(ic == 0), stop=(ic == EC - 1))
                            return ins
                        C.step("pe", pmm, reads=[wsl[sl], lnb_b], writes=[pj_pb[ng]])
                        C.step("dve", lambda e, oc=oc, ng=ng, dbuf=dbuf, t0=t0:
                               e.tensor_copy(
                                   dbuf[:, oc * S + t0 + ng * NG:oc * S + t0 + (ng + 1) * NG],
                                   pj_ps[:, ng * NG:(ng + 1) * NG]),
                               reads=[pj_pb[ng]], writes=[lat])
                else:
                    for ng in range(2):
                        def kmm(e, sl=sl, ng=ng, lnb=lnb):
                            for ic in range(EC):
                                ins = e.matmul(
                                    kr_ps[:, ng * NG:(ng + 1) * NG],
                                    wbuf[:, sl * 2048 + ic * 128:sl * 2048 + (ic + 1) * 128],
                                    lnb[:, ic * TT + ng * NG:ic * TT + (ng + 1) * NG],
                                    start=(ic == 0), stop=(ic == EC - 1))
                            return ins
                        C.step("pe", kmm, reads=[wsl[sl], lnb_b], writes=[kr_pb])

                    def rope(e, oc=oc, dbuf=dbuf, t0=t0):
                        e.tensor_tensor(rop1[:, 0:TT], kr_ps[:, 0:TT],
                                        g["cosT"][:, t0:t0 + TT], ALU.mult)
                        for g4 in range(4):
                            src0 = (g4 + 1) * 32 if g4 % 2 == 0 else (g4 - 1) * 32
                            e.tensor_tensor(
                                rop2[g4 * 32:(g4 + 1) * 32, 0:TT],
                                kr_ps[src0:src0 + 32, 0:TT],
                                g["sinT"][g4 * 32:(g4 + 1) * 32, t0:t0 + TT],
                                ALU.mult)
                        return e.tensor_add(
                            dbuf[:, oc * S + t0:oc * S + t0 + TT],
                            rop1[:, 0:TT], rop2[:, 0:TT])
                    C.step("dve", rope, reads=[kr_pb, consts_of(g)],
                           writes=[lat, ropb, kr_pb])
                if j == CQC + CKVC - 1 and tile + 1 < NTILE:
                    _ln_compute(nc, C, lnbs[(tile + 1) % 2],
                                lnb_bs[(tile + 1) % 2], sqb, sqsl, tmpb, tmsl,
                                rowset, pj_ps, pj_pb, g)


def _attention(nc, C, sb, pst, c_q, c_kv, k_r, lat,
               wuqT, wqrT, wukT, wuvT, g, ao_s, tag):
    with ExitStack() as stA:
        kcT = sb(tag + "_kc", [128, S], BF16, stA)
        vkm = sb(tag + "_v", [128, S], BF16, stA)
        qcT = sb(tag + "_qc", [128, S], BF16, stA)
        qrT = sb(tag + "_qr", [128, S], BF16, stA)
        ebuf = sb(tag + "_e", [128, (S // 128) * NG], BF16, stA)
        wuq_b = sb(tag + "_wuq", [128, 2 * CQC * 128], BF16, stA)
        wqr_b = sb(tag + "_wqr", [128, 2 * CQC * 64], BF16, stA)
        wuk_b = sb(tag + "_wuk", [128, 2 * CKVC * 128], BF16, stA)
        wuv_b = sb(tag + "_wuv", [128, 2 * CKVC * 128], BF16, stA)
        rop1 = sb(tag + "_r1", [64, NG], F32, stA)
        rop2 = sb(tag + "_r2", [64, NG], F32, stA)
        masks = sb(tag + "_mask", [128, 4 * NG], BF16, stA)
        masks_b = Buf()
        C.step("pl", lambda e: e.dma_start(
            out=masks.ap().rearrange("p (j n) -> p j n", j=4),
            in_=g["mask_in"].rearrange("j p n -> p j n")), writes=[masks_b])
        lnz = sb(tag + "_lnz", [1, NG], F32, stA)
        recr = sb(tag + "_recr", [1, NG], F32, stA)
        recb = sb(tag + "_recb", [1, 2 * NG], BF16, stA)
        recf = sb(tag + "_recf", [128, 2 * NG], BF16, stA)
        aout = sb(tag + "_ao", [128, 2 * NG], BF16, stA)

        wq_sl = [Buf(), Buf()]
        kc_b = Buf()
        v_b = Buf()
        qc_b = Buf()
        qr_b = Buf()
        ropb = Buf()
        e_b = [Buf() for _ in range(S // 128)]
        rec_b = Buf()
        recb_b = [Buf(), Buf()]
        recf_b = [Buf(), Buf()]
        ao_b = [Buf(), Buf()]

        with ExitStack() as stP:
            sc_ps = [pst(tag + "_sc0", [128, NG], F32, stP),
                     pst(tag + "_sc1", [128, NG], F32, stP)]
            pv_ps = [pst(tag + "_pv0", [128, NG], F32, stP),
                     pst(tag + "_pv1", [128, NG], F32, stP)]
            z_ps = [pst(tag + "_z0", [1, NG], F32, stP),
                    pst(tag + "_z1", [1, NG], F32, stP)]
            bc_ps = pst(tag + "_bc", [128, NG], F32, stP)
            qr_ps = pst(tag + "_qrp", [64, NG], F32, stP)
            sc_pb = [Buf(), Buf()]
            pv_pb = [Buf(), Buf()]
            z_pb = [Buf(), Buf()]
            bc_pb = Buf()
            qr_pb = Buf()

            pend = []

            def emit_tail(h, qg, qp):
                C.step("pe", lambda e, qp=qp: e.matmul(
                    bc_ps[:], g["oner"][:], recb[0:1, qp * NG:(qp + 1) * NG],
                    start=True, stop=True),
                    reads=[recb_b[qp], consts_of(g)], writes=[bc_pb])
                C.step("act", lambda e, qp=qp: e.activation(
                    recf[:, qp * NG:(qp + 1) * NG], bc_ps[:], AF.Copy),
                    reads=[bc_pb], writes=[recf_b[qp]])
                C.step("dve", lambda e, qp=qp: e.tensor_tensor(
                    aout[:, qp * NG:(qp + 1) * NG], pv_ps[qp][:],
                    recf[:, qp * NG:(qp + 1) * NG], ALU.mult),
                    reads=[pv_pb[qp], recf_b[qp]], writes=[ao_b[qp]])
                C.step("pl", lambda e, h=h, qg=qg, qp=qp: e.dma_start(
                    out=ao_s[h][:, qg * NG:(qg + 1) * NG],
                    in_=aout[:, qp * NG:(qp + 1) * NG]),
                    reads=[ao_b[qp]], dsem="st%d" % qp)

            for h in range(H):
                sl = h % 2
                krbase = 64 * (h % 2)
                krcol = (h // 2) * S

                def wload(e, h=h, sl=sl):
                    r = []
                    r.append(e.dma_start(
                        out=wuq_b.ap()[:, sl * CQC * 128:(sl + 1) * CQC * 128].rearrange(
                            "p (a n) -> p a n", a=CQC),
                        in_=wuqT.rearrange("(a p) m -> p a m", p=128)[
                            :, :, h * 128:(h + 1) * 128]))
                    r.append(e.dma_start(
                        out=wqr_b.ap()[:, sl * CQC * 64:(sl + 1) * CQC * 64].rearrange(
                            "p (a n) -> p a n", a=CQC),
                        in_=wqrT.rearrange("(a p) m -> p a m", p=128)[
                            :, :, h * 64:(h + 1) * 64]))
                    r.append(e.dma_start(
                        out=wuk_b.ap()[:, sl * CKVC * 128:(sl + 1) * CKVC * 128].rearrange(
                            "p (a n) -> p a n", a=CKVC),
                        in_=wukT.rearrange("(a p) m -> p a m", p=128)[
                            :, :, h * 128:(h + 1) * 128]))
                    r.append(e.dma_start(
                        out=wuv_b.ap()[:, sl * CKVC * 128:(sl + 1) * CKVC * 128].rearrange(
                            "p (a n) -> p a n", a=CKVC),
                        in_=wuvT.rearrange("(a p) m -> p a m", p=128)[
                            :, :, h * 128:(h + 1) * 128]))
                    return r
                C.step("pl", wload, writes=[wq_sl[sl]], inc=4, dsem="g%d" % sl)

                for ng in range(NQG):
                    pp = ng % 2

                    def kcmm(e, sl=sl, ng=ng, pp=pp):
                        for ic in range(CKVC):
                            ins = e.matmul(
                                sc_ps[pp][:],
                                wuk_b[:, sl * CKVC * 128 + ic * 128:sl * CKVC * 128 + (ic + 1) * 128],
                                c_kv[:, ic * S + ng * NG:ic * S + (ng + 1) * NG],
                                start=(ic == 0), stop=(ic == CKVC - 1))
                        return ins
                    C.step("pe", kcmm, reads=[wq_sl[sl], lat], writes=[sc_pb[pp]])
                    C.step("dve", lambda e, ng=ng, pp=pp: e.tensor_copy(
                        kcT[:, ng * NG:(ng + 1) * NG], sc_ps[pp][:]),
                        reads=[sc_pb[pp]], writes=[kc_b])
                for ng in range(NQG):
                    pp = ng % 2

                    def vmm(e, sl=sl, ng=ng, pp=pp):
                        for tc4 in range(4):
                            tc = ng * 4 + tc4
                            for ic in range(CKVC):
                                ins = e.matmul(
                                    sc_ps[pp][:, tc4 * 128:(tc4 + 1) * 128],
                                    c_kv[:, ic * S + tc * 128:ic * S + (tc + 1) * 128],
                                    wuv_b[:, sl * CKVC * 128 + ic * 128:sl * CKVC * 128 + (ic + 1) * 128],
                                    start=(ic == 0), stop=(ic == CKVC - 1))
                        return ins
                    C.step("pe", vmm, reads=[wq_sl[sl], lat], writes=[sc_pb[pp]])
                    C.step("dve", lambda e, ng=ng, pp=pp: e.tensor_copy(
                        vkm[:, ng * NG:(ng + 1) * NG], sc_ps[pp][:]),
                        reads=[sc_pb[pp]], writes=[v_b])
                for ng in range(NQG):
                    pp = ng % 2

                    def qcmm(e, sl=sl, ng=ng, pp=pp):
                        for ic in range(CQC):
                            ins = e.matmul(
                                sc_ps[pp][:],
                                wuq_b[:, sl * CQC * 128 + ic * 128:sl * CQC * 128 + (ic + 1) * 128],
                                c_q[:, ic * S + ng * NG:ic * S + (ng + 1) * NG],
                                start=(ic == 0), stop=(ic == CQC - 1))
                        return ins
                    C.step("pe", qcmm, reads=[wq_sl[sl], lat], writes=[sc_pb[pp]])
                    C.step("dve", lambda e, ng=ng, pp=pp: e.tensor_copy(
                        qcT[:, ng * NG:(ng + 1) * NG], sc_ps[pp][:]),
                        reads=[sc_pb[pp]], writes=[qc_b])
                for ng in range(NQG):
                    def qrmm(e, sl=sl, ng=ng):
                        for ic in range(CQC):
                            ins = e.matmul(
                                qr_ps[:],
                                wqr_b[:, sl * CQC * 64 + ic * 64:sl * CQC * 64 + (ic + 1) * 64],
                                c_q[:, ic * S + ng * NG:ic * S + (ng + 1) * NG],
                                start=(ic == 0), stop=(ic == CQC - 1))
                        return ins
                    C.step("pe", qrmm, reads=[wq_sl[sl], lat], writes=[qr_pb])

                    def qrope(e, ng=ng, krbase=krbase):
                        e.tensor_tensor(rop1[:, 0:NG], qr_ps[:],
                                        g["cosT"][0:64, ng * NG:(ng + 1) * NG], ALU.mult)
                        for g2 in range(2):
                            src0 = 32 if g2 == 0 else 0
                            e.tensor_tensor(
                                rop2[g2 * 32:(g2 + 1) * 32, 0:NG],
                                qr_ps[src0:src0 + 32, :],
                                g["sinT"][g2 * 32:(g2 + 1) * 32, ng * NG:(ng + 1) * NG],
                                ALU.mult)
                        return e.tensor_add(qrT[krbase:krbase + 64, ng * NG:(ng + 1) * NG],
                                            rop1[:, 0:NG], rop2[:, 0:NG])
                    C.step("dve", qrope, reads=[qr_pb, consts_of(g)],
                           writes=[qr_b, ropb, qr_pb])

                for qg in range(NQG):
                    nkc = (qg + 1) * KCPG
                    order = list(range(nkc - KCPG, nkc)) + list(range(nkc - KCPG))
                    for kc in order:
                        pp = kc % 2

                        def smm(e, kc=kc, pp=pp, qg=qg, krbase=krbase, krcol=krcol):
                            e.matmul(sc_ps[pp][:],
                                     kcT[:, kc * 128:(kc + 1) * 128],
                                     qcT[:, qg * NG:(qg + 1) * NG],
                                     start=True, stop=False)
                            return e.matmul(
                                sc_ps[pp][:],
                                k_r[krbase:krbase + 64,
                                    krcol + kc * 128:krcol + (kc + 1) * 128],
                                qrT[krbase:krbase + 64, qg * NG:(qg + 1) * NG],
                                start=False, stop=True)
                        C.step("pe", smm, reads=[kc_b, qc_b, qr_b, lat],
                               writes=[sc_pb[pp]])

                        C.step("act", lambda e, kc=kc, pp=pp: e.activation(
                            ebuf[:, kc * NG:(kc + 1) * NG], sc_ps[pp][:],
                            AF.Exp, bias=g["c_shift"][:], scale=SCALE),
                            reads=[sc_pb[pp], consts_of(g)], writes=[e_b[kc]])
                        if kc >= nkc - KCPG:
                            jj = kc - (nkc - KCPG)
                            C.step("dve", lambda e, kc=kc, jj=jj: e.tensor_tensor(
                                ebuf[:, kc * NG:(kc + 1) * NG],
                                ebuf[:, kc * NG:(kc + 1) * NG],
                                masks[:, jj * NG:(jj + 1) * NG], ALU.mult),
                                reads=[e_b[kc], masks_b], writes=[e_b[kc]])

                    qp = qg % 2

                    def pvmm(e, qg=qg, nkc=nkc, qp=qp):
                        for kc in range(nkc):
                            ins = e.matmul(pv_ps[qp][:],
                                           vkm[:, kc * 128:(kc + 1) * 128],
                                           ebuf[:, kc * NG:(kc + 1) * NG],
                                           start=(kc == 0), stop=(kc == nkc - 1))
                        return ins
                    C.step("pe", pvmm, reads=[v_b] + [e_b[kc] for kc in range(nkc)],
                           writes=[pv_pb[qp]])

                    def zmm(e, qg=qg, nkc=nkc, qp=qp):
                        for kc in range(nkc):
                            ins = e.matmul(z_ps[qp][:], g["onec"][:],
                                           ebuf[:, kc * NG:(kc + 1) * NG],
                                           start=(kc == 0), stop=(kc == nkc - 1))
                        return ins
                    C.step("pe", zmm,
                           reads=[consts_of(g)] + [e_b[kc] for kc in range(nkc)],
                           writes=[z_pb[qp]])

                    def recip(e, qp=qp):
                        e.activation(lnz[:], z_ps[qp][:], AF.Ln)
                        return e.activation(recr[:], lnz[:], AF.Exp, scale=-1.0)
                    C.step("act", recip, reads=[z_pb[qp]], writes=[rec_b])
                    C.step("dve", lambda e, qp=qp: e.tensor_copy(
                        recb[0:1, qp * NG:(qp + 1) * NG], recr[:]),
                        reads=[rec_b], writes=[recb_b[qp]])

                    if pend:
                        emit_tail(*pend.pop(0))
                    pend.append((h, qg, qp))
            while pend:
                emit_tail(*pend.pop(0))
            C.soft_barrier()


def _outproj(nc, C, sb, pst, ao_s, woutT, x1_s, x2_s, g, tag, x1_db, x2_db):
    with ExitStack() as stO:
        aob = sb(tag + "_ao", [128, H * S], BF16, stO)
        wob = sb(tag + "_w", [128, 2 * 2048], BF16, stO)
        xep = sb(tag + "_x", [128, 4 * NG], F32, stO)
        x2o = sb(tag + "_o", [128, 4 * NG], F32, stO)
        aob_b = Buf()
        wsl = [Buf(), Buf()]
        xsl = [Buf() for _ in range(4)]
        osl = [Buf() for _ in range(4)]
        with ExitStack() as stP:
            o_ps = pst(tag + "_ps", [128, 4 * NG], F32, stP)
            o_pb = [Buf() for _ in range(4)]

            C.step("pl", lambda e: e.dma_start(
                out=aob.ap().rearrange("p (a n) -> p a n", a=H),
                in_=ao_s.rearrange("h p n -> p h n")), writes=[aob_b])

            for og in range(EC):
                sl = og % 2
                C.step("pl", lambda e, og=og, sl=sl: e.dma_start(
                    out=wob.ap()[:, sl * 2048:(sl + 1) * 2048].rearrange(
                        "p (a n) -> p a n", a=EC),
                    in_=woutT.rearrange("(a p) m -> p a m", p=128)[
                        :, :, og * 128:(og + 1) * 128]),
                    writes=[wsl[sl]], dsem="g%d" % sl)
                for ng in range(NQG):
                    C.step("sp", lambda e, og=og, ng=ng: e.dma_start(
                        out=xep[:, ng * NG:(ng + 1) * NG],
                        in_=x1_s[og * 128:(og + 1) * 128, ng * NG:(ng + 1) * NG]),
                        writes=[xsl[ng]])

                    def omm(e, og=og, sl=sl, ng=ng):
                        for ic in range(H):
                            ins = e.matmul(
                                o_ps[:, ng * NG:(ng + 1) * NG],
                                wob[:, sl * 2048 + ic * 128:sl * 2048 + (ic + 1) * 128],
                                aob[:, ic * S + ng * NG:ic * S + (ng + 1) * NG],
                                start=(ic == 0), stop=(ic == H - 1))
                        return ins
                    C.step("pe", omm, reads=[wsl[sl], aob_b], writes=[o_pb[ng]])

                    C.step("dve", lambda e, og=og, ng=ng: e.tensor_tensor(
                        x2o[:, ng * NG:(ng + 1) * NG],
                        o_ps[:, ng * NG:(ng + 1) * NG],
                        xep[:, ng * NG:(ng + 1) * NG], ALU.add),
                        reads=[o_pb[ng], xsl[ng]], writes=[osl[ng]])

                    C.step("pl", lambda e, og=og, ng=ng: e.dma_start(
                        out=x2_s[og * 128:(og + 1) * 128, ng * NG:(ng + 1) * NG],
                        in_=x2o[:, ng * NG:(ng + 1) * NG]),
                        reads=[osl[ng]], writes=[x2_db[ng // 2]],
                        dsem="st%d" % (ng % 2))



def _host_constants():
    t = np.arange(S, dtype=np.float64)
    inv = 1.0 / (10000.0 ** (np.arange(0, RD, 2, dtype=np.float64) / RD))
    fr = np.outer(inv, t)
    cos32 = np.cos(fr).astype(np.float32)
    sin32 = np.sin(fr).astype(np.float32)
    cosT = np.tile(np.concatenate([cos32, cos32], axis=0), (2, 1))
    sinS = np.tile(np.concatenate([-sin32, sin32], axis=0), (2, 1))
    p = np.arange(128)[:, None]
    f = np.arange(NG)[None, :]
    masks = np.stack([(f >= 128 * j + p).astype(np.float32) for j in range(4)])
    return (np.ascontiguousarray(cosT).astype(bfdt),
            np.ascontiguousarray(sinS).astype(bfdt),
            np.ascontiguousarray(masks).astype(bfdt),
            np.ones((128, 1), np.float32).astype(bfdt),
            np.ones((1, 128), np.float32).astype(bfdt))


def prep_weights(d):
    c = np.ascontiguousarray
    return dict(
        wdqT=c(np.asarray(d["L0_attn_w_dq"], np.float32).T),
        wuqT=c(np.asarray(d["L0_attn_w_uq"], np.float32).T),
        wdkvT=c(np.asarray(d["L0_attn_w_dkv"], np.float32).T),
        wukT=c(np.asarray(d["L0_attn_w_uk"], np.float32).T),
        wuvT=c(np.asarray(d["L0_attn_w_uv"], np.float32).T),
        wqrT=c(np.asarray(d["L0_attn_w_qr"], np.float32).T),
        wkrT=c(np.asarray(d["L0_attn_w_kr"], np.float32).T),
        woutT=c(np.asarray(d["L0_attn_w_out"], np.float32).T),
        wgT=c(np.asarray(d["L0_ffn_wq_gate"], np.int32).T),
        wuT=c(np.asarray(d["L0_ffn_wq_up"], np.int32).T),
        wdT=c(np.asarray(d["L0_ffn_wq_down"], np.int32).T),
        sg=c(np.asarray(d["L0_ffn_s_gate"], np.float32).reshape(HIDC, 128).T),
        su=c(np.asarray(d["L0_ffn_s_up"], np.float32).reshape(HIDC, 128).T),
        sd=c(np.asarray(d["L0_ffn_s_down"], np.float32).reshape(EC, 128).T),
    )


def make_in_map(core, x_full, w, nseq=NSEQ):
    cosT, sinS, masks, onesc, onesr = _host_constants()
    xs = np.ascontiguousarray(
        np.transpose(x_full[core * NSEQ:core * NSEQ + nseq], (0, 2, 1)))
    return dict(
        x_in=xs,
        wdqT=w["wdqT"], wuqT=w["wuqT"], wdkvT=w["wdkvT"], wukT=w["wukT"],
        wuvT=w["wuvT"], wqrT=w["wqrT"], wkrT=w["wkrT"], woutT=w["woutT"],
        wgT=w["wgT"], wuT=w["wuT"], wdT=w["wdT"],
        sg_in=w["sg"], su_in=w["su"], sd_in=w["sd"],
        cos_in=cosT, sin_in=sinS, mask_in=masks, ones_c=onesc, ones_r=onesr,
    )


def kernel(**inputs):
    x_full = np.asarray(inputs["L0_attn_x"], np.float32)
    w = prep_weights(inputs)
    nc = build_core_program()
    in_maps = [make_in_map(c, x_full, w) for c in range(8)]
    res = run_bass_kernel_spmd(nc, in_maps, core_ids=list(range(8)))
    outs = [np.transpose(res.results[c]["y_out"], (0, 2, 1)) for c in range(8)]
    return np.ascontiguousarray(np.concatenate(outs, axis=0))


# version 16
# speedup vs baseline: 1.0832x; 1.0832x over previous
import numpy as np
import ml_dtypes
import concourse.bass as bass
import concourse.mybir as mybir
from concourse.bass_utils import run_bass_kernel_spmd
from contextlib import ExitStack

F32 = mybir.dt.float32
BF16 = mybir.dt.bfloat16
I32 = mybir.dt.int32
AF = mybir.ActivationFunctionType
ALU = mybir.AluOpType
bfdt = ml_dtypes.bfloat16

S = 2048
E = 2048
EC = 16
CQ = 1536
CQC = 12
CKV = 512
CKVC = 4
RD = 64
H = 16
HD = 128
HID = 5632
HIDC = 44
NSEQ = 2
TT = 1024
NSUB = 2
NTILE = S // TT
NG = 512
SCALE = float(1.0 / np.sqrt(HD + RD))
ESHIFT = -12.0
EPS = 1e-5
QG = 512
NQG = S // QG
KCPG = QG // 128


class Buf:

    def __init__(self):
        self.reads = []
        self.wr = None

    def for_write(self):
        w = list(self.reads)
        if self.wr is not None:
            w.append(self.wr)
        return w

    def for_read(self):
        return [self.wr] if self.wr is not None else []

    def did_write(self, sem, val):
        self.reads = []
        self.wr = (sem, val)

    def did_read(self, sem, val):
        self.reads.append((sem, val))


class Ctx:

    ENGS = ("pe", "act", "dve", "sp", "pl")
    DMA_SEMS = ("sp", "pl", "st0", "st1", "lnb", "g0", "g1")
    ALL_SEMS = ENGS + ("st0", "st1", "lnb", "g0", "g1")

    def __init__(self, nc, es):
        self.nc = nc
        self.sem = {n: es.enter_context(nc.semaphore("sem_" + n))
                    for n in self.ALL_SEMS}
        self.cnt = {n: 0 for n in self.ALL_SEMS}
        self.lastw = {}
        self.q = {n: [] for n in self.ENGS}

    def step(self, ename, emit, reads=(), writes=(), inc=1, dsem=None):
        sname = dsem or ename
        waits = []
        for b in reads:
            waits += b.for_read()
        for b in writes:
            waits += b.for_write()
        wlist = []
        need = {}
        for dep, val in waits:
            if dep == sname:
                continue
            assert val <= self.cnt[dep], (ename, dep, val, self.cnt[dep])
            if dep in self.DMA_SEMS:
                val = self.cnt[dep]
            need[dep] = max(need.get(dep, 0), val)
        for dep, val in need.items():
            if self.lastw.get((ename, dep), -1) < val:
                self.lastw[(ename, dep)] = val
                wlist.append((dep, val))
        n = 16 if sname in self.DMA_SEMS else 1
        self.cnt[sname] += n * inc
        mark = self.cnt[sname]
        sem = self.sem[sname]

        def run(eng, wlist=wlist, emit=emit, sem=sem, n=n, inc=inc):
            for dep, val in wlist:
                eng.wait_ge(self.sem[dep], val)
            last = emit(eng)
            if inc:
                assert last is not None
                if isinstance(last, (list, tuple)):
                    assert len(last) == inc
                    for one in last:
                        one.then_inc(sem, n)
                else:
                    last.then_inc(sem, n * inc)

        self.q[ename].append(run)
        for b in reads:
            b.did_read(sname, mark)
        for b in writes:
            b.did_write(sname, mark)
        return mark

    def raw_wait(self, ename, dep=None, val=None):
        dep = dep or ename
        val = self.cnt[dep] if val is None else val

        def run(eng, dep=dep, val=val):
            eng.wait_ge(self.sem[dep], val)
        self.q[ename].append(run)
        self.lastw[(ename, dep)] = max(self.lastw.get((ename, dep), -1), val)

    SOFT = False

    def soft_barrier(self):
        if not self.SOFT:
            self.flush()
            return
        for e in self.ENGS:
            for d in self.ALL_SEMS:
                if d == e:
                    continue
                v = self.cnt[d]
                if v > 0 and self.lastw.get((e, d), -1) < v:
                    self.raw_wait(e, d, v)

    def flush(self):
        qs = self.q
        self.q = {n: [] for n in self.ENGS}
        if not any(qs.values()):
            return
        for dma in self.DMA_SEMS:
            v = self.cnt[dma]
            if v > 0:
                tgt = "pl" if qs["pl"] else "sp"

                def fin(eng, dma=dma, v=v):
                    eng.wait_ge(self.sem[dma], v)
                qs[tgt].append(fin)

        def mk(name):
            def body(eng):
                for fn in qs[name]:
                    fn(eng)
            return body

        with self.nc.Block() as blk:
            if qs["pe"]:
                blk.tensor(mk("pe"))
            if qs["act"]:
                blk.scalar(mk("act"))
            if qs["dve"]:
                blk.vector(mk("dve"))
            if qs["sp"]:
                blk.sync(mk("sp"))
            if qs["pl"]:
                blk.gpsimd(mk("pl"))
        for e in self.ENGS:
            for nm in self.ALL_SEMS:
                self.lastw[(e, nm)] = max(self.lastw.get((e, nm), -1), self.cnt[nm])



def build_core_program(dumps=(), stop_after=None, nseq=NSEQ):
    nc = bass.Bass("TRN2", target_bir_lowering=False)

    x_in = nc.dram_tensor("x_in", [nseq, E, S], F32, kind="ExternalInput").ap()
    wdqT = nc.dram_tensor("wdqT", [E, CQ], F32, kind="ExternalInput").ap()
    wuqT = nc.dram_tensor("wuqT", [CQ, E], F32, kind="ExternalInput").ap()
    wdkvT = nc.dram_tensor("wdkvT", [E, CKV], F32, kind="ExternalInput").ap()
    wukT = nc.dram_tensor("wukT", [CKV, E], F32, kind="ExternalInput").ap()
    wuvT = nc.dram_tensor("wuvT", [CKV, E], F32, kind="ExternalInput").ap()
    wqrT = nc.dram_tensor("wqrT", [CQ, H * RD], F32, kind="ExternalInput").ap()
    wkrT = nc.dram_tensor("wkrT", [E, H * RD], F32, kind="ExternalInput").ap()
    woutT = nc.dram_tensor("woutT", [E, E], F32, kind="ExternalInput").ap()
    wgT = nc.dram_tensor("wgT", [E, HID], I32, kind="ExternalInput").ap()
    wuT = nc.dram_tensor("wuT", [E, HID], I32, kind="ExternalInput").ap()
    wdT = nc.dram_tensor("wdT", [HID, E], I32, kind="ExternalInput").ap()
    sg_in = nc.dram_tensor("sg_in", [128, HIDC], F32, kind="ExternalInput").ap()
    su_in = nc.dram_tensor("su_in", [128, HIDC], F32, kind="ExternalInput").ap()
    sd_in = nc.dram_tensor("sd_in", [128, EC], F32, kind="ExternalInput").ap()
    cos_in = nc.dram_tensor("cos_in", [128, S], BF16, kind="ExternalInput").ap()
    sin_in = nc.dram_tensor("sin_in", [128, S], BF16, kind="ExternalInput").ap()
    mask_in = nc.dram_tensor("mask_in", [4, 128, NG], BF16, kind="ExternalInput").ap()
    ones_c = nc.dram_tensor("ones_c", [128, 1], BF16, kind="ExternalInput").ap()
    ones_r = nc.dram_tensor("ones_r", [1, 128], BF16, kind="ExternalInput").ap()

    y_out = nc.dram_tensor("y_out", [nseq, E, S], F32, kind="ExternalOutput").ap()

    wg_s = nc.dram_tensor("wg_s", [HIDC, 128, EC * 128], BF16, kind="Internal").ap()
    wu_s = nc.dram_tensor("wu_s", [HIDC, 128, EC * 128], BF16, kind="Internal").ap()
    wd_s = nc.dram_tensor("wd_s", [EC, 128, HIDC * 128], BF16, kind="Internal").ap()
    x1_s = nc.dram_tensor("x1_s", [E, S], F32, kind="Internal").ap()
    x2_s = nc.dram_tensor("x2_s", [E, S], F32, kind="Internal").ap()
    ao_s = nc.dram_tensor("ao_s", [H, HD, S], BF16, kind="Internal").ap()

    dump_aps = {}
    for name in dumps:
        src = {"x1": x1_s, "x2": x2_s, "ao": ao_s}[name]
        dump_aps[name] = (nc.dram_tensor("dump_" + name, list(src.shape),
                                         src.dtype, kind="ExternalOutput").ap(), src)

    es = ExitStack()
    with es:
        C = Ctx(nc, es)

        def sb(name, shape, dtype, stack=es):
            return stack.enter_context(nc.sbuf_tensor(name, shape, dtype))

        def pst(name, shape, dtype=F32, stack=es):
            return stack.enter_context(nc.psum_tensor(name, shape, dtype))

        g = dict(
            cosT=sb("cosT", [128, S], BF16), sinT=sb("sinT", [128, S], BF16),
            sg=sb("sg", [128, HIDC], F32), su=sb("su", [128, HIDC], F32),
            sdh=sb("sdh", [128, EC], F32),
            onec=sb("onec", [128, 1], BF16), oner=sb("oner", [1, 128], BF16),
            c_eps=sb("c_eps", [1, 1], F32), c_shift=sb("c_shift", [128, 1], F32),
        )
        lnA = sb("lnA", [128, EC * TT], BF16)
        lnA_b = Buf()
        g["mask_in"] = mask_in

        consts = Buf()
        for dst, src in ((g["cosT"], cos_in), (g["sinT"], sin_in),
                         (g["sg"], sg_in), (g["su"], su_in), (g["sdh"], sd_in),
                         (g["onec"], ones_c), (g["oner"], ones_r)):
            C.step("sp", lambda e, dst=dst, src=src: e.dma_start(out=dst[:], in_=src),
                   writes=[consts])
        C.step("dve", lambda e: e.tensor_scalar_mul(g["sdh"][:], g["sdh"][:], 0.5),
               reads=[consts], writes=[consts])
        C.step("dve", lambda e: e.memset(g["c_eps"][:], EPS), writes=[consts])
        C.step("dve", lambda e: e.memset(g["c_shift"][:], ESHIFT), writes=[consts])
        C.flush()

        with ExitStack() as st0:
            p0i = sb("p0_i", [128, 4 * 2048], I32, st0)
            p0b = sb("p0_b", [128, 4 * 2048], BF16, st0)
            p0is = [Buf() for _ in range(4)]
            p0bs = [Buf() for _ in range(4)]
            jobs0 = ([("wg", wgT, wg_s, oc, EC) for oc in range(HIDC)] +
                     [("wu", wuT, wu_s, oc, EC) for oc in range(HIDC)] +
                     [("wd", wdT, wd_s, oc, HIDC) for oc in range(EC)])
            jctr = 0
            for kind0, srcw0, dstw0, oc0, nic0 in jobs0:
                for c00 in range(0, nic0, 16):
                    n0 = min(16, nic0 - c00)
                    sl0 = jctr % 4
                    q0 = ("sp", "pl")[jctr % 2]
                    jctr += 1
                    X0 = n0 * 128
                    C.step(q0, lambda e, srcw0=srcw0, oc0=oc0, sl0=sl0, X0=X0, n0=n0, c00=c00:
                           e.dma_start(out=p0i.ap()[:, sl0 * 2048:sl0 * 2048 + X0].rearrange(
                                           "p (a n) -> p a n", a=n0),
                                       in_=srcw0.rearrange("(a p) m -> p a m", p=128)[
                                           :, c00:c00 + n0, oc0 * 128:(oc0 + 1) * 128]),
                           writes=[p0is[sl0]], dsem="g%d" % (sl0 % 2))
                    C.step("dve", lambda e, sl0=sl0, X0=X0:
                           e.tensor_scalar_add(p0b[:, sl0 * 2048:sl0 * 2048 + X0],
                                               p0i[:, sl0 * 2048:sl0 * 2048 + X0], -128.0),
                           reads=[p0is[sl0]], writes=[p0bs[sl0]])
                    C.step(("pl", "sp")[jctr % 2],
                           lambda e, dstw0=dstw0, oc0=oc0, sl0=sl0, X0=X0, c00=c00:
                           e.dma_start(out=dstw0[oc0][:, c00 * 128:c00 * 128 + X0],
                                       in_=p0b[:, sl0 * 2048:sl0 * 2048 + X0]),
                           reads=[p0bs[sl0]], dsem="st%d" % (sl0 % 2))
            C.soft_barrier()

        x1_db = [Buf() for _ in range(NTILE)]
        x2_db = [Buf() for _ in range(NTILE)]
        _ln_load(nc, C, lnA, lnA_b, x_in[0], 0)
        for s in range(nseq):
            _ffn_pass(nc, C, sb, pst, x_in[s], x1_s, wg_s, wu_s, wd_s, g,
                      "f1s%d" % s, lnA, lnA_b, x1_db, dq=None)
            _ln_load(nc, C, lnA, lnA_b, x1_s, 0, src_b=x1_db[0])
            C.soft_barrier()
            if stop_after == "P1" and s == nseq - 1:
                _finish(nc, C, dump_aps)
                return nc

            with ExitStack() as st23:
                c_q = sb("cq%d" % s, [128, CQC * S], BF16, st23)
                c_kv = sb("ckv%d" % s, [128, CKVC * S], BF16, st23)
                k_r = sb("kr%d" % s, [128, (H // 2) * S], BF16, st23)
                lat = Buf()
                _proj_pass(nc, C, sb, pst, x1_s, c_q, c_kv, k_r, lat,
                           wdqT, wdkvT, wkrT, g, "p2s%d" % s, lnA, lnA_b, x1_db)
                C.soft_barrier()
                if stop_after == "P2" and s == nseq - 1:
                    _finish(nc, C, dump_aps,
                            extra=[("c_q", c_q), ("c_kv", c_kv), ("k_r", k_r)])
                    return nc
                _attention(nc, C, sb, pst, c_q, c_kv, k_r, lat,
                           wuqT, wqrT, wukT, wuvT, g, ao_s, "p3s%d" % s)
            if stop_after == "P3" and s == nseq - 1:
                _finish(nc, C, dump_aps)
                return nc

            _outproj(nc, C, sb, pst, ao_s, woutT, x1_s, x2_s, g, "oas%d" % s,
                     x1_db, x2_db)
            _ln_load(nc, C, lnA, lnA_b, x2_s, 0, src_b=x2_db[0])
            C.soft_barrier()
            if stop_after == "P4a" and s == nseq - 1:
                _finish(nc, C, dump_aps)
                return nc

            _ffn_pass(nc, C, sb, pst, x2_s, y_out[s], wg_s, wu_s, wd_s, g,
                      "f2s%d" % s, lnA, lnA_b, None)
            if s + 1 < nseq:
                _ln_load(nc, C, lnA, lnA_b, x_in[s + 1], 0)
            C.soft_barrier()

        _finish(nc, C, dump_aps)
    return nc


def _finish(nc, C, dump_aps, extra=None):
    for name, (dap, src) in dump_aps.items():
        C.step("pl", lambda e, dap=dap, src=src: e.dma_start(out=dap, in_=src))
    if extra:
        for nm, t in extra:
            dap = nc.dram_tensor("dumpsb_" + nm, list(t.ap().shape),
                                 t.ap().dtype, kind="ExternalOutput").ap()
            C.step("pl", lambda e, dap=dap, t=t: e.dma_start(out=dap, in_=t[:]))
    C.raw_wait("pl")
    C.raw_wait("sp")
    C.flush()


def _ln_load(nc, C, lnb, lnb_b, src_view, t0, src_b=None):
    for sub in range(NSUB):
        sl = lnb.ap().rearrange("p (c t) -> p c t", c=EC)[:, :, sub * NG:(sub + 1) * NG]
        C.step("pl", lambda e, sl=sl, sub=sub: e.dma_start(
            out=sl,
            in_=src_view.rearrange("(c p) t -> p c t", p=128)[
                :, :, t0 + sub * NG:t0 + (sub + 1) * NG]),
            reads=([src_b] if src_b is not None else []), writes=[lnb_b],
            dsem="lnb")


def _ln_compute(nc, C, lnb, lnb_b, sqb, sqb_b, tmpb, tmpb_b, rowset, sps, spb, g):
    rb, rowbufs = rowset
    for sub in range(NSUB):
        s0 = sub * NG
        for c in range(EC):
            csl = lnb[:, c * TT + s0:c * TT + s0 + NG]
            ssl = sqb[:, (c % 2) * NG:(c % 2 + 1) * NG]
            C.step("dve", lambda e, csl=csl, ssl=ssl: e.tensor_mul(ssl, csl, csl),
                   reads=[lnb_b], writes=[sqb_b[c % 2]])

            def mm(e, c=c, csl=csl, ssl=ssl):
                e.matmul(sps[0:1, 0:NG], g["onec"][:], csl,
                         start=(c == 0), stop=(c == EC - 1))
                return e.matmul(sps[0:1, NG:2 * NG], g["onec"][:], ssl,
                                start=(c == 0), stop=(c == EC - 1))
            C.step("pe", mm, reads=[lnb_b, sqb_b[c % 2], consts_of(g)],
                   writes=[spb[0], spb[1]])
        n = 1.0 / E

        def rows(e, sub=sub):
            e.tensor_scalar_mul(rb["mean"][0:1, sub * NG:(sub + 1) * NG],
                                sps[0:1, 0:NG], n)
            e.tensor_scalar_mul(rb["ex2"][0:1, sub * NG:(sub + 1) * NG],
                                sps[0:1, NG:2 * NG], n)
            e.tensor_mul(rb["msq"][0:1, sub * NG:(sub + 1) * NG],
                         rb["mean"][0:1, sub * NG:(sub + 1) * NG],
                         rb["mean"][0:1, sub * NG:(sub + 1) * NG])
            return e.tensor_sub(rb["var"][0:1, sub * NG:(sub + 1) * NG],
                                rb["ex2"][0:1, sub * NG:(sub + 1) * NG],
                                rb["msq"][0:1, sub * NG:(sub + 1) * NG])
        C.step("dve", rows, reads=[spb[0], spb[1]], writes=[rowbufs])

    def rstd(e):
        e.activation(rb["lnv"][:], rb["var"][:], AF.Ln, bias=g["c_eps"][:])
        return e.activation(rb["rstd"][:], rb["lnv"][:], AF.Exp, scale=-0.5)
    C.step("act", rstd, reads=[rowbufs, consts_of(g)], writes=[rowbufs])

    def tobf(e):
        e.tensor_copy(rb["mean_b"][:], rb["mean"][:])
        return e.tensor_copy(rb["rstd_b"][:], rb["rstd"][:])
    C.step("dve", tobf, reads=[rowbufs], writes=[rowbufs])

    for sub in range(NSUB):
        s0 = sub * NG

        def bcast(e, sub=sub):
            e.matmul(sps[:, 0:NG], g["oner"][:],
                     rb["mean_b"][0:1, sub * NG:(sub + 1) * NG],
                     start=True, stop=True)
            return e.matmul(sps[:, NG:2 * NG], g["oner"][:],
                            rb["rstd_b"][0:1, sub * NG:(sub + 1) * NG],
                            start=True, stop=True)
        C.step("pe", bcast, reads=[rowbufs, consts_of(g)], writes=[spb[0], spb[1]])

        for c in range(EC):
            csl = lnb[:, c * TT + s0:c * TT + s0 + NG]
            tsl = tmpb[:, (c % 2) * NG:(c % 2 + 1) * NG]

            def norm(e, csl=csl, tsl=tsl):
                e.tensor_tensor(tsl, csl, sps[:, 0:NG], ALU.subtract)
                return e.tensor_tensor(csl, tsl, sps[:, NG:2 * NG], ALU.mult)
            C.step("dve", norm, reads=[spb[0], spb[1], lnb_b],
                   writes=[lnb_b, tmpb_b[c % 2]])


_CONSTS_BUF = {}


def consts_of(g):
    key = id(g)
    if key not in _CONSTS_BUF:
        _CONSTS_BUF[key] = Buf()
    return _CONSTS_BUF[key]


def _mk_rowset(sb, stack, tag):
    W = NSUB * NG
    R = sb(tag + "_rows", [1, 3 * W], F32, stack)
    B = sb(tag + "_rowsb", [1, 2 * W], BF16, stack)
    rb = dict(
        mean=R[0:1, 0:W], ex2=R[0:1, W:2 * W], msq=R[0:1, 2 * W:3 * W],
        var=R[0:1, W:2 * W], lnv=R[0:1, 2 * W:3 * W], rstd=R[0:1, W:2 * W],
        mean_b=B[0:1, 0:W], rstd_b=B[0:1, W:2 * W],
    )
    return rb, Buf()


def _ffn_pass(nc, C, sb, pst, src_dram, dst_dram, wg_s, wu_s, wd_s, g, tag,
              lnA, lnA_b, dst_db, dq=None):
    with ExitStack() as stF:
        lnb, lnb_b = lnA, lnA_b
        hbuf = sb(tag + "_h", [128, HIDC * TT], BF16, stF)
        wgbuf = sb(tag + "_wg", [128, 2 * 2048], BF16, stF)
        wubuf = sb(tag + "_wu", [128, 2 * 2048], BF16, stF)
        wdbuf = sb(tag + "_wd", [128, 2 * HID], BF16, stF)
        gbuf = sb(tag + "_g", [128, 2 * NG], BF16, stF)
        sqb = sb(tag + "_sq", [128, 2 * NG], BF16, stF)
        tmpb = sb(tag + "_tmp", [128, 2 * NG], BF16, stF)
        xep = sb(tag + "_xep", [128, 2 * NG], F32, stF)
        x1o = sb(tag + "_x1o", [128, 2 * NG], F32, stF)
        rowset = _mk_rowset(sb, stF, tag)
        if dq:
            wgT_, wuT_, wdT_ = dq
            ibuf = sb(tag + "_ib", [128, 2 * 1024], I32, stF)
            isl = [Buf(), Buf()]
            wdst = sb(tag + "_wds", [128, 1024], BF16, stF)
            wdsl_st = [Buf()]

        h_b = [Buf() for _ in range(HIDC * 2)]
        wgsl = [Buf(), Buf()]
        wusl = [Buf(), Buf()]
        wdsl = [Buf(), Buf()]
        gsl = [Buf(), Buf()]
        sqsl = [Buf(), Buf()]
        tmsl = [Buf(), Buf()]
        xsl = [Buf(), Buf()]
        osl = [Buf(), Buf()]

        gate_ps = pst(tag + "_gp", [128, TT], F32, stF)
        up_ps = pst(tag + "_up", [128, TT], F32, stF)
        down_ps = pst(tag + "_dp", [128, TT], F32, stF)
        gate_pb = [Buf(), Buf()]
        up_pb = [Buf(), Buf()]
        down_pb = [Buf(), Buf()]

        _ln_compute(nc, C, lnb, lnb_b, sqb, sqsl, tmpb, tmsl, rowset,
                    gate_ps, gate_pb, g)

        wdram = {}
        wd_done = [0]
        dqctr = [0]

        def dq_chain(kind, srcw, scratch, oc, wbuf_, wsl_, sl, nic, c0=0):
            X = nic * 128
            W = 2048 if kind != "wd" else HID
            il = dqctr[0] % 2
            dqctr[0] += 1
            C.step("sp", lambda e: e.dma_start(
                out=ibuf.ap()[:, il * 1024:il * 1024 + X].rearrange(
                    "p (a n) -> p a n", a=nic),
                in_=srcw.rearrange("(a p) m -> p a m", p=128)[
                    :, c0:c0 + nic, oc * 128:(oc + 1) * 128]),
                writes=[isl[il]], dsem="g%d" % il)
            if wbuf_ is None:
                st = 0

                C.step("dve", lambda e: e.tensor_scalar_add(
                    wdst[:, 0:X],
                    ibuf[:, il * 1024:il * 1024 + X], -128.0),
                    reads=[isl[il]], writes=[wdsl_st[0]])
                db = wdram.setdefault((kind, oc), Buf())
                C.step("pl", lambda e: e.dma_start(
                    out=scratch[oc][:, c0 * 128:c0 * 128 + X],
                    in_=wdst[:, 0:X]),
                    reads=[wdsl_st[0]], writes=[db], dsem="st0")
                return
            C.step("dve", lambda e: e.tensor_scalar_add(
                wbuf_[:, sl * W + c0 * 128:sl * W + c0 * 128 + X],
                ibuf[:, il * 1024:il * 1024 + X], -128.0),
                reads=[isl[il]], writes=[wsl_])
            db = wdram.setdefault((kind, oc), Buf())
            C.step("pl", lambda e: e.dma_start(
                out=scratch[oc][:, c0 * 128:c0 * 128 + X],
                in_=wbuf_[:, sl * W + c0 * 128:sl * W + c0 * 128 + X]),
                reads=[wsl_], writes=[db], dsem="st%d" % sl)

        for tile in range(NTILE):
            t0 = tile * TT
            for oc in range(HIDC):
                sl = oc % 2

                if dq and tile == 0:
                    dq_chain("wg", wgT_, wg_s, oc, wgbuf, wgsl[sl], sl, 8, 0)
                    dq_chain("wg", wgT_, wg_s, oc, wgbuf, wgsl[sl], sl, 8, 8)
                    dq_chain("wu", wuT_, wu_s, oc, wubuf, wusl[sl], sl, 8, 0)
                    dq_chain("wu", wuT_, wu_s, oc, wubuf, wusl[sl], sl, 8, 8)
                    wd_subs = [(og, c0, nic) for og in range(EC)
                               for c0, nic in ((0, 8), (8, 8), (16, 8),
                                               (24, 8), (32, 8), (40, 4))]
                    while wd_done[0] < (oc + 1) * len(wd_subs) // HIDC:
                        og, c0, nic = wd_subs[wd_done[0]]
                        dq_chain("wd", wdT_, wd_s, og, None, None, 0, nic, c0)
                        wd_done[0] += 1
                else:
                    def wload(e, oc=oc, sl=sl):
                        a = e.dma_start(out=wgbuf[:, sl * 2048:(sl + 1) * 2048],
                                        in_=wg_s[oc])
                        b = e.dma_start(out=wubuf[:, sl * 2048:(sl + 1) * 2048],
                                        in_=wu_s[oc])
                        return [a, b]
                    rds = [wdram[k] for k in (("wg", oc), ("wu", oc)) if k in wdram]
                    C.step("sp", wload, reads=rds,
                           writes=[wgsl[sl], wusl[sl]], inc=2)

                for ng in range(2):
                    def gmm(e, oc=oc, sl=sl, ng=ng):
                        for ic in range(EC):
                            ins = e.matmul(
                                gate_ps[:, ng * NG:(ng + 1) * NG],
                                wgbuf[:, sl * 2048 + ic * 128:sl * 2048 + (ic + 1) * 128],
                                lnb[:, ic * TT + ng * NG:ic * TT + (ng + 1) * NG],
                                start=(ic == 0), stop=(ic == EC - 1))
                        return ins
                    C.step("pe", gmm, reads=[wgsl[sl], lnb_b], writes=[gate_pb[ng]])

                    def umm(e, oc=oc, sl=sl, ng=ng):
                        for ic in range(EC):
                            ins = e.matmul(
                                up_ps[:, ng * NG:(ng + 1) * NG],
                                wubuf[:, sl * 2048 + ic * 128:sl * 2048 + (ic + 1) * 128],
                                lnb[:, ic * TT + ng * NG:ic * TT + (ng + 1) * NG],
                                start=(ic == 0), stop=(ic == EC - 1))
                        return ins
                    C.step("pe", umm, reads=[wusl[sl], lnb_b], writes=[up_pb[ng]])

                    C.step("act", lambda e, oc=oc, ng=ng: e.activation(
                        gbuf[:, ng * NG:(ng + 1) * NG],
                        gate_ps[:, ng * NG:(ng + 1) * NG],
                        AF.Silu, scale=g["sg"][:, oc:oc + 1]),
                        reads=[gate_pb[ng], consts_of(g)], writes=[gsl[ng]])

                    C.step("dve", lambda e, oc=oc, ng=ng: e.scalar_tensor_tensor(
                        hbuf[:, oc * TT + ng * NG:oc * TT + (ng + 1) * NG],
                        up_ps[:, ng * NG:(ng + 1) * NG],
                        g["su"][:, oc:oc + 1],
                        gbuf[:, ng * NG:(ng + 1) * NG],
                        ALU.mult, ALU.mult),
                        reads=[up_pb[ng], gsl[ng], consts_of(g)],
                        writes=[h_b[oc * 2 + ng]])

            if tile + 1 < NTILE:
                _ln_load(nc, C, lnb, lnb_b, src_dram, t0 + TT)

            for og in range(EC):
                sl = og % 2
                rds = [wdram[k] for k in (("wd", og),) if k in wdram]
                C.step("sp", lambda e, og=og, sl=sl: e.dma_start(
                    out=wdbuf[:, sl * HID:(sl + 1) * HID], in_=wd_s[og]),
                    reads=rds, writes=[wdsl[sl]])
                for ng in range(2):
                    xslot = ng
                    C.step("sp", lambda e, og=og, ng=ng, xslot=xslot, t0=t0:
                           e.dma_start(
                               out=xep[:, xslot * NG:(xslot + 1) * NG],
                               in_=src_dram[og * 128:(og + 1) * 128,
                                            t0 + ng * NG:t0 + (ng + 1) * NG]),
                           writes=[xsl[xslot]])

                    def dmms(e, og=og, sl=sl, ng=ng):
                        for hc in range(HIDC):
                            ins = e.matmul(
                                down_ps[:, ng * NG:(ng + 1) * NG],
                                wdbuf[:, sl * HID + hc * 128:sl * HID + (hc + 1) * 128],
                                hbuf[:, hc * TT + ng * NG:hc * TT + (ng + 1) * NG],
                                start=(hc == 0), stop=(hc == HIDC - 1))
                        return ins
                    C.step("pe", dmms,
                           reads=[wdsl[sl]] + [h_b[hc * 2 + ng] for hc in range(HIDC)],
                           writes=[down_pb[ng]])

                    C.step("dve", lambda e, og=og, ng=ng, xslot=xslot:
                           e.scalar_tensor_tensor(
                               x1o[:, xslot * NG:(xslot + 1) * NG],
                               down_ps[:, ng * NG:(ng + 1) * NG],
                               g["sdh"][:, og:og + 1],
                               xep[:, xslot * NG:(xslot + 1) * NG],
                               ALU.mult, ALU.add),
                           reads=[down_pb[ng], xsl[xslot], consts_of(g)],
                           writes=[osl[xslot]])

                    C.step("pl", lambda e, og=og, ng=ng, xslot=xslot, t0=t0:
                           e.dma_start(
                               out=dst_dram[og * 128:(og + 1) * 128,
                                            t0 + ng * NG:t0 + (ng + 1) * NG],
                               in_=x1o[:, xslot * NG:(xslot + 1) * NG]),
                           reads=[osl[xslot]],
                           writes=([dst_db[tile]] if dst_db else []),
                           dsem="st%d" % ng)
                if og == 7 and tile + 1 < NTILE:
                    _ln_compute(nc, C, lnb, lnb_b, sqb, sqsl, tmpb, tmsl,
                                rowset, gate_ps, gate_pb, g)


def _proj_pass(nc, C, sb, pst, x1_s, c_q, c_kv, k_r, lat,
               wdqT, wdkvT, wkrT, g, tag, lnA, lnA_b, x1_db):
    with ExitStack() as stP:
        lnbs = [lnA, sb(tag + "_ln1", [128, EC * TT], BF16, stP)]
        wbuf = sb(tag + "_w", [128, 2 * 2048], BF16, stP)
        sqb = sb(tag + "_sq", [128, 2 * NG], BF16, stP)
        tmpb = sb(tag + "_tmp", [128, 2 * NG], BF16, stP)
        rop1 = sb(tag + "_r1", [128, TT], F32, stP)
        rop2 = sb(tag + "_r2", [128, TT], F32, stP)
        rowset = _mk_rowset(sb, stP, tag)
        lnb_bs = [lnA_b, Buf()]
        wsl = [Buf(), Buf()]
        sqsl = [Buf(), Buf()]
        tmsl = [Buf(), Buf()]
        ropb = Buf()

        pj_ps = pst(tag + "_pj", [128, 2 * NG], F32, stP)
        kr_ps = pst(tag + "_kr", [128, TT], F32, stP)
        pj_pb = [Buf(), Buf()]
        kr_pb = Buf()

        _ln_compute(nc, C, lnbs[0], lnb_bs[0], sqb, sqsl, tmpb, tmsl, rowset,
                    pj_ps, pj_pb, g)

        for tile in range(NTILE):
            t0 = tile * TT
            lnb = lnbs[tile % 2]
            lnb_b = lnb_bs[tile % 2]
            if tile + 1 < NTILE:
                _ln_load(nc, C, lnbs[(tile + 1) % 2], lnb_bs[(tile + 1) % 2],
                         x1_s, t0 + TT, src_b=x1_db[tile + 1])
            jobs = ([("cq", wdqT, oc, c_q) for oc in range(CQC)] +
                    [("ckv", wdkvT, oc, c_kv) for oc in range(CKVC)] +
                    [("kr", wkrT, oc, k_r) for oc in range(H // 2)])
            for j, (kind, wsrc, oc, dbuf) in enumerate(jobs):
                sl = j % 2
                C.step("pl", lambda e, wsrc=wsrc, oc=oc, sl=sl: e.dma_start(
                    out=wbuf.ap()[:, sl * 2048:(sl + 1) * 2048].rearrange(
                        "p (a n) -> p a n", a=EC),
                    in_=wsrc.rearrange("(a p) m -> p a m", p=128)[
                        :, :, oc * 128:(oc + 1) * 128]),
                    writes=[wsl[sl]], dsem="g%d" % sl)

                if kind != "kr":
                    for ng in range(2):
                        def pmm(e, sl=sl, ng=ng, lnb=lnb):
                            for ic in range(EC):
                                ins = e.matmul(
                                    pj_ps[:, ng * NG:(ng + 1) * NG],
                                    wbuf[:, sl * 2048 + ic * 128:sl * 2048 + (ic + 1) * 128],
                                    lnb[:, ic * TT + ng * NG:ic * TT + (ng + 1) * NG],
                                    start=(ic == 0), stop=(ic == EC - 1))
                            return ins
                        C.step("pe", pmm, reads=[wsl[sl], lnb_b], writes=[pj_pb[ng]])
                        C.step("dve", lambda e, oc=oc, ng=ng, dbuf=dbuf, t0=t0:
                               e.tensor_copy(
                                   dbuf[:, oc * S + t0 + ng * NG:oc * S + t0 + (ng + 1) * NG],
                                   pj_ps[:, ng * NG:(ng + 1) * NG]),
                               reads=[pj_pb[ng]], writes=[lat])
                else:
                    for ng in range(2):
                        def kmm(e, sl=sl, ng=ng, lnb=lnb):
                            for ic in range(EC):
                                ins = e.matmul(
                                    kr_ps[:, ng * NG:(ng + 1) * NG],
                                    wbuf[:, sl * 2048 + ic * 128:sl * 2048 + (ic + 1) * 128],
                                    lnb[:, ic * TT + ng * NG:ic * TT + (ng + 1) * NG],
                                    start=(ic == 0), stop=(ic == EC - 1))
                            return ins
                        C.step("pe", kmm, reads=[wsl[sl], lnb_b], writes=[kr_pb])

                    def rope(e, oc=oc, dbuf=dbuf, t0=t0):
                        e.tensor_tensor(rop1[:, 0:TT], kr_ps[:, 0:TT],
                                        g["cosT"][:, t0:t0 + TT], ALU.mult)
                        for g4 in range(4):
                            src0 = (g4 + 1) * 32 if g4 % 2 == 0 else (g4 - 1) * 32
                            e.tensor_tensor(
                                rop2[g4 * 32:(g4 + 1) * 32, 0:TT],
                                kr_ps[src0:src0 + 32, 0:TT],
                                g["sinT"][g4 * 32:(g4 + 1) * 32, t0:t0 + TT],
                                ALU.mult)
                        return e.tensor_add(
                            dbuf[:, oc * S + t0:oc * S + t0 + TT],
                            rop1[:, 0:TT], rop2[:, 0:TT])
                    C.step("dve", rope, reads=[kr_pb, consts_of(g)],
                           writes=[lat, ropb, kr_pb])
                if j == CQC + CKVC - 1 and tile + 1 < NTILE:
                    _ln_compute(nc, C, lnbs[(tile + 1) % 2],
                                lnb_bs[(tile + 1) % 2], sqb, sqsl, tmpb, tmsl,
                                rowset, pj_ps, pj_pb, g)


def _attention(nc, C, sb, pst, c_q, c_kv, k_r, lat,
               wuqT, wqrT, wukT, wuvT, g, ao_s, tag):
    with ExitStack() as stA:
        kcT = sb(tag + "_kc", [128, S], BF16, stA)
        vkm = sb(tag + "_v", [128, S], BF16, stA)
        qcT = sb(tag + "_qc", [128, S], BF16, stA)
        qrT = sb(tag + "_qr", [128, S], BF16, stA)
        ebuf = sb(tag + "_e", [128, (S // 128) * NG], BF16, stA)
        wuq_b = sb(tag + "_wuq", [128, 2 * CQC * 128], BF16, stA)
        wqr_b = sb(tag + "_wqr", [128, 2 * CQC * 64], BF16, stA)
        wuk_b = sb(tag + "_wuk", [128, 2 * CKVC * 128], BF16, stA)
        wuv_b = sb(tag + "_wuv", [128, 2 * CKVC * 128], BF16, stA)
        rop1 = sb(tag + "_r1", [64, NG], F32, stA)
        rop2 = sb(tag + "_r2", [64, NG], F32, stA)
        masks = sb(tag + "_mask", [128, 4 * NG], BF16, stA)
        masks_b = Buf()
        C.step("pl", lambda e: e.dma_start(
            out=masks.ap().rearrange("p (j n) -> p j n", j=4),
            in_=g["mask_in"].rearrange("j p n -> p j n")), writes=[masks_b])
        lnz = sb(tag + "_lnz", [1, NG], F32, stA)
        recr = sb(tag + "_recr", [1, NG], F32, stA)
        recb = sb(tag + "_recb", [1, 2 * NG], BF16, stA)
        recf = sb(tag + "_recf", [128, 2 * NG], BF16, stA)
        aout = sb(tag + "_ao", [128, 2 * NG], BF16, stA)

        wq_sl = [Buf(), Buf()]
        kc_b = Buf()
        v_b = Buf()
        qc_b = Buf()
        qr_b = Buf()
        ropb = Buf()
        e_b = [Buf() for _ in range(S // 128)]
        rec_b = Buf()
        recb_b = [Buf(), Buf()]
        recf_b = [Buf(), Buf()]
        ao_b = [Buf(), Buf()]

        with ExitStack() as stP:
            sc_ps = [pst(tag + "_sc0", [128, NG], F32, stP),
                     pst(tag + "_sc1", [128, NG], F32, stP)]
            pv_ps = [pst(tag + "_pv0", [128, NG], F32, stP),
                     pst(tag + "_pv1", [128, NG], F32, stP)]
            z_ps = [pst(tag + "_z0", [1, NG], F32, stP)]
            bc_ps = pst(tag + "_bc", [128, NG], F32, stP)
            qr_ps = [pst(tag + "_qr0", [64, NG], F32, stP),
                     pst(tag + "_qr1", [64, NG], F32, stP)]
            sc_pb = [Buf(), Buf()]
            pv_pb = [Buf(), Buf()]
            z_pb = [Buf()]
            bc_pb = Buf()
            qr_pb = [Buf(), Buf()]

            pend = []

            def emit_tail(h, qg, qp):
                C.step("pe", lambda e, qp=qp: e.matmul(
                    bc_ps[:], g["oner"][:], recb[0:1, qp * NG:(qp + 1) * NG],
                    start=True, stop=True),
                    reads=[recb_b[qp], consts_of(g)], writes=[bc_pb])
                C.step("dve", lambda e, qp=qp: e.tensor_copy(
                    recf[:, qp * NG:(qp + 1) * NG], bc_ps[:]),
                    reads=[bc_pb], writes=[recf_b[qp]])
                C.step("dve", lambda e, qp=qp: e.tensor_tensor(
                    aout[:, qp * NG:(qp + 1) * NG], pv_ps[qp][:],
                    recf[:, qp * NG:(qp + 1) * NG], ALU.mult),
                    reads=[pv_pb[qp], recf_b[qp]], writes=[ao_b[qp]])
                C.step("pl", lambda e, h=h, qg=qg, qp=qp: e.dma_start(
                    out=ao_s[h][:, qg * NG:(qg + 1) * NG],
                    in_=aout[:, qp * NG:(qp + 1) * NG]),
                    reads=[ao_b[qp]], dsem="st%d" % qp)

            for h in range(H):
                sl = h % 2
                krbase = 64 * (h % 2)
                krcol = (h // 2) * S

                def wload(e, h=h, sl=sl):
                    r = []
                    r.append(e.dma_start(
                        out=wuq_b.ap()[:, sl * CQC * 128:(sl + 1) * CQC * 128].rearrange(
                            "p (a n) -> p a n", a=CQC),
                        in_=wuqT.rearrange("(a p) m -> p a m", p=128)[
                            :, :, h * 128:(h + 1) * 128]))
                    r.append(e.dma_start(
                        out=wqr_b.ap()[:, sl * CQC * 64:(sl + 1) * CQC * 64].rearrange(
                            "p (a n) -> p a n", a=CQC),
                        in_=wqrT.rearrange("(a p) m -> p a m", p=128)[
                            :, :, h * 64:(h + 1) * 64]))
                    r.append(e.dma_start(
                        out=wuk_b.ap()[:, sl * CKVC * 128:(sl + 1) * CKVC * 128].rearrange(
                            "p (a n) -> p a n", a=CKVC),
                        in_=wukT.rearrange("(a p) m -> p a m", p=128)[
                            :, :, h * 128:(h + 1) * 128]))
                    r.append(e.dma_start(
                        out=wuv_b.ap()[:, sl * CKVC * 128:(sl + 1) * CKVC * 128].rearrange(
                            "p (a n) -> p a n", a=CKVC),
                        in_=wuvT.rearrange("(a p) m -> p a m", p=128)[
                            :, :, h * 128:(h + 1) * 128]))
                    return r
                C.step("pl", wload, writes=[wq_sl[sl]], inc=4, dsem="g%d" % sl)

                for ng in range(NQG):
                    pp = ng % 2

                    def kcmm(e, sl=sl, ng=ng, pp=pp):
                        for ic in range(CKVC):
                            ins = e.matmul(
                                sc_ps[pp][:],
                                wuk_b[:, sl * CKVC * 128 + ic * 128:sl * CKVC * 128 + (ic + 1) * 128],
                                c_kv[:, ic * S + ng * NG:ic * S + (ng + 1) * NG],
                                start=(ic == 0), stop=(ic == CKVC - 1))
                        return ins
                    C.step("pe", kcmm, reads=[wq_sl[sl], lat], writes=[sc_pb[pp]])
                    C.step("dve", lambda e, ng=ng, pp=pp: e.tensor_copy(
                        kcT[:, ng * NG:(ng + 1) * NG], sc_ps[pp][:]),
                        reads=[sc_pb[pp]], writes=[kc_b])
                for ng in range(NQG):
                    pp = ng % 2

                    def vmm(e, sl=sl, ng=ng, pp=pp):
                        for tc4 in range(4):
                            tc = ng * 4 + tc4
                            for ic in range(CKVC):
                                ins = e.matmul(
                                    sc_ps[pp][:, tc4 * 128:(tc4 + 1) * 128],
                                    c_kv[:, ic * S + tc * 128:ic * S + (tc + 1) * 128],
                                    wuv_b[:, sl * CKVC * 128 + ic * 128:sl * CKVC * 128 + (ic + 1) * 128],
                                    start=(ic == 0), stop=(ic == CKVC - 1))
                        return ins
                    C.step("pe", vmm, reads=[wq_sl[sl], lat], writes=[sc_pb[pp]])
                    C.step("dve", lambda e, ng=ng, pp=pp: e.tensor_copy(
                        vkm[:, ng * NG:(ng + 1) * NG], sc_ps[pp][:]),
                        reads=[sc_pb[pp]], writes=[v_b])
                for ng in range(NQG):
                    pp = ng % 2

                    def qcmm(e, sl=sl, ng=ng, pp=pp):
                        for ic in range(CQC):
                            ins = e.matmul(
                                sc_ps[pp][:],
                                wuq_b[:, sl * CQC * 128 + ic * 128:sl * CQC * 128 + (ic + 1) * 128],
                                c_q[:, ic * S + ng * NG:ic * S + (ng + 1) * NG],
                                start=(ic == 0), stop=(ic == CQC - 1))
                        return ins
                    C.step("pe", qcmm, reads=[wq_sl[sl], lat], writes=[sc_pb[pp]])
                    C.step("dve", lambda e, ng=ng, pp=pp: e.tensor_copy(
                        qcT[:, ng * NG:(ng + 1) * NG], sc_ps[pp][:]),
                        reads=[sc_pb[pp]], writes=[qc_b])

                    def qrmm(e, sl=sl, ng=ng, pp=pp):
                        for ic in range(CQC):
                            ins = e.matmul(
                                qr_ps[pp][:],
                                wqr_b[:, sl * CQC * 64 + ic * 64:sl * CQC * 64 + (ic + 1) * 64],
                                c_q[:, ic * S + ng * NG:ic * S + (ng + 1) * NG],
                                start=(ic == 0), stop=(ic == CQC - 1))
                        return ins
                    C.step("pe", qrmm, reads=[wq_sl[sl], lat], writes=[qr_pb[pp]])

                    def qrope(e, ng=ng, pp=pp, krbase=krbase):
                        e.tensor_tensor(rop1[:, 0:NG], qr_ps[pp][:],
                                        g["cosT"][0:64, ng * NG:(ng + 1) * NG], ALU.mult)
                        for g2 in range(2):
                            src0 = 32 if g2 == 0 else 0
                            e.tensor_tensor(
                                rop2[g2 * 32:(g2 + 1) * 32, 0:NG],
                                qr_ps[pp][src0:src0 + 32, :],
                                g["sinT"][g2 * 32:(g2 + 1) * 32, ng * NG:(ng + 1) * NG],
                                ALU.mult)
                        return e.tensor_add(qrT[krbase:krbase + 64, ng * NG:(ng + 1) * NG],
                                            rop1[:, 0:NG], rop2[:, 0:NG])
                    C.step("dve", qrope, reads=[qr_pb[pp], consts_of(g)],
                           writes=[qr_b, ropb, qr_pb[pp]])

                for qg in range(NQG):
                    nkc = (qg + 1) * KCPG
                    order = list(range(nkc - KCPG, nkc)) + list(range(nkc - KCPG))
                    for kc in order:
                        pp = kc % 2

                        def smm(e, kc=kc, pp=pp, qg=qg, krbase=krbase, krcol=krcol):
                            e.matmul(
                                sc_ps[pp][:],
                                k_r[krbase:krbase + 64,
                                    krcol + kc * 128:krcol + (kc + 1) * 128],
                                qrT[krbase:krbase + 64, qg * NG:(qg + 1) * NG],
                                start=True, stop=False)
                            return e.matmul(sc_ps[pp][:],
                                            kcT[:, kc * 128:(kc + 1) * 128],
                                            qcT[:, qg * NG:(qg + 1) * NG],
                                            start=False, stop=True)
                        C.step("pe", smm, reads=[kc_b, qc_b, qr_b, lat],
                               writes=[sc_pb[pp]])

                        C.step("act", lambda e, kc=kc, pp=pp: e.activation(
                            ebuf[:, kc * NG:(kc + 1) * NG], sc_ps[pp][:],
                            AF.Exp, bias=g["c_shift"][:], scale=SCALE),
                            reads=[sc_pb[pp], consts_of(g)], writes=[e_b[kc]])
                        if kc >= nkc - KCPG:
                            jj = kc - (nkc - KCPG)
                            C.step("dve", lambda e, kc=kc, jj=jj: e.tensor_tensor(
                                ebuf[:, kc * NG:(kc + 1) * NG],
                                ebuf[:, kc * NG:(kc + 1) * NG],
                                masks[:, jj * NG:(jj + 1) * NG], ALU.mult),
                                reads=[e_b[kc], masks_b], writes=[e_b[kc]])

                    qp = qg % 2

                    def zmm(e, qg=qg, nkc=nkc):
                        for kc in range(nkc):
                            ins = e.matmul(z_ps[0][:], g["onec"][:],
                                           ebuf[:, kc * NG:(kc + 1) * NG],
                                           start=(kc == 0), stop=(kc == nkc - 1))
                        return ins
                    C.step("pe", zmm,
                           reads=[consts_of(g)] + [e_b[kc] for kc in range(nkc)],
                           writes=[z_pb[0]])

                    def pvmm(e, qg=qg, nkc=nkc, qp=qp):
                        for kc in range(nkc):
                            ins = e.matmul(pv_ps[qp][:],
                                           vkm[:, kc * 128:(kc + 1) * 128],
                                           ebuf[:, kc * NG:(kc + 1) * NG],
                                           start=(kc == 0), stop=(kc == nkc - 1))
                        return ins
                    C.step("pe", pvmm, reads=[v_b] + [e_b[kc] for kc in range(nkc)],
                           writes=[pv_pb[qp]])

                    def recip(e, qp=qp):
                        e.activation(lnz[:], z_ps[0][:], AF.Ln)
                        return e.activation(recr[:], lnz[:], AF.Exp, scale=-1.0)
                    C.step("act", recip, reads=[z_pb[0]], writes=[rec_b])
                    C.step("dve", lambda e, qp=qp: e.tensor_copy(
                        recb[0:1, qp * NG:(qp + 1) * NG], recr[:]),
                        reads=[rec_b], writes=[recb_b[qp]])

                    if pend:
                        emit_tail(*pend.pop(0))
                    pend.append((h, qg, qp))
            while pend:
                emit_tail(*pend.pop(0))
            C.soft_barrier()


def _outproj(nc, C, sb, pst, ao_s, woutT, x1_s, x2_s, g, tag, x1_db, x2_db):
    with ExitStack() as stO:
        aob = sb(tag + "_ao", [128, H * S], BF16, stO)
        wob = sb(tag + "_w", [128, 2 * 2048], BF16, stO)
        xep = sb(tag + "_x", [128, 4 * NG], F32, stO)
        x2o = sb(tag + "_o", [128, 4 * NG], F32, stO)
        aob_b = Buf()
        wsl = [Buf(), Buf()]
        xsl = [Buf() for _ in range(4)]
        osl = [Buf() for _ in range(4)]
        with ExitStack() as stP:
            o_ps = pst(tag + "_ps", [128, 4 * NG], F32, stP)
            o_pb = [Buf() for _ in range(4)]

            C.step("pl", lambda e: e.dma_start(
                out=aob.ap().rearrange("p (a n) -> p a n", a=H),
                in_=ao_s.rearrange("h p n -> p h n")), writes=[aob_b])

            for og in range(EC):
                sl = og % 2
                C.step("pl", lambda e, og=og, sl=sl: e.dma_start(
                    out=wob.ap()[:, sl * 2048:(sl + 1) * 2048].rearrange(
                        "p (a n) -> p a n", a=EC),
                    in_=woutT.rearrange("(a p) m -> p a m", p=128)[
                        :, :, og * 128:(og + 1) * 128]),
                    writes=[wsl[sl]], dsem="g%d" % sl)
                for ng in range(NQG):
                    C.step("sp", lambda e, og=og, ng=ng: e.dma_start(
                        out=xep[:, ng * NG:(ng + 1) * NG],
                        in_=x1_s[og * 128:(og + 1) * 128, ng * NG:(ng + 1) * NG]),
                        writes=[xsl[ng]])

                    def omm(e, og=og, sl=sl, ng=ng):
                        for ic in range(H):
                            ins = e.matmul(
                                o_ps[:, ng * NG:(ng + 1) * NG],
                                wob[:, sl * 2048 + ic * 128:sl * 2048 + (ic + 1) * 128],
                                aob[:, ic * S + ng * NG:ic * S + (ng + 1) * NG],
                                start=(ic == 0), stop=(ic == H - 1))
                        return ins
                    C.step("pe", omm, reads=[wsl[sl], aob_b], writes=[o_pb[ng]])

                    C.step("dve", lambda e, og=og, ng=ng: e.tensor_tensor(
                        x2o[:, ng * NG:(ng + 1) * NG],
                        o_ps[:, ng * NG:(ng + 1) * NG],
                        xep[:, ng * NG:(ng + 1) * NG], ALU.add),
                        reads=[o_pb[ng], xsl[ng]], writes=[osl[ng]])

                    C.step("pl", lambda e, og=og, ng=ng: e.dma_start(
                        out=x2_s[og * 128:(og + 1) * 128, ng * NG:(ng + 1) * NG],
                        in_=x2o[:, ng * NG:(ng + 1) * NG]),
                        reads=[osl[ng]], writes=[x2_db[ng // 2]],
                        dsem="st%d" % (ng % 2))



def _host_constants():
    t = np.arange(S, dtype=np.float64)
    inv = 1.0 / (10000.0 ** (np.arange(0, RD, 2, dtype=np.float64) / RD))
    fr = np.outer(inv, t)
    cos32 = np.cos(fr).astype(np.float32)
    sin32 = np.sin(fr).astype(np.float32)
    cosT = np.tile(np.concatenate([cos32, cos32], axis=0), (2, 1))
    sinS = np.tile(np.concatenate([-sin32, sin32], axis=0), (2, 1))
    p = np.arange(128)[:, None]
    f = np.arange(NG)[None, :]
    masks = np.stack([(f >= 128 * j + p).astype(np.float32) for j in range(4)])
    return (np.ascontiguousarray(cosT).astype(bfdt),
            np.ascontiguousarray(sinS).astype(bfdt),
            np.ascontiguousarray(masks).astype(bfdt),
            np.ones((128, 1), np.float32).astype(bfdt),
            np.ones((1, 128), np.float32).astype(bfdt))


def prep_weights(d):
    c = np.ascontiguousarray
    return dict(
        wdqT=c(np.asarray(d["L0_attn_w_dq"], np.float32).T),
        wuqT=c(np.asarray(d["L0_attn_w_uq"], np.float32).T),
        wdkvT=c(np.asarray(d["L0_attn_w_dkv"], np.float32).T),
        wukT=c(np.asarray(d["L0_attn_w_uk"], np.float32).T),
        wuvT=c(np.asarray(d["L0_attn_w_uv"], np.float32).T),
        wqrT=c(np.asarray(d["L0_attn_w_qr"], np.float32).T),
        wkrT=c(np.asarray(d["L0_attn_w_kr"], np.float32).T),
        woutT=c(np.asarray(d["L0_attn_w_out"], np.float32).T),
        wgT=c(np.asarray(d["L0_ffn_wq_gate"], np.int32).T),
        wuT=c(np.asarray(d["L0_ffn_wq_up"], np.int32).T),
        wdT=c(np.asarray(d["L0_ffn_wq_down"], np.int32).T),
        sg=c(np.asarray(d["L0_ffn_s_gate"], np.float32).reshape(HIDC, 128).T),
        su=c(np.asarray(d["L0_ffn_s_up"], np.float32).reshape(HIDC, 128).T),
        sd=c(np.asarray(d["L0_ffn_s_down"], np.float32).reshape(EC, 128).T),
    )


def make_in_map(core, x_full, w, nseq=NSEQ):
    cosT, sinS, masks, onesc, onesr = _host_constants()
    xs = np.ascontiguousarray(
        np.transpose(x_full[core * NSEQ:core * NSEQ + nseq], (0, 2, 1)))
    return dict(
        x_in=xs,
        wdqT=w["wdqT"], wuqT=w["wuqT"], wdkvT=w["wdkvT"], wukT=w["wukT"],
        wuvT=w["wuvT"], wqrT=w["wqrT"], wkrT=w["wkrT"], woutT=w["woutT"],
        wgT=w["wgT"], wuT=w["wuT"], wdT=w["wdT"],
        sg_in=w["sg"], su_in=w["su"], sd_in=w["sd"],
        cos_in=cosT, sin_in=sinS, mask_in=masks, ones_c=onesc, ones_r=onesr,
    )


def kernel(**inputs):
    x_full = np.asarray(inputs["L0_attn_x"], np.float32)
    w = prep_weights(inputs)
    nc = build_core_program()
    in_maps = [make_in_map(c, x_full, w) for c in range(8)]
    res = run_bass_kernel_spmd(nc, in_maps, core_ids=list(range(8)))
    outs = [np.transpose(res.results[c]["y_out"], (0, 2, 1)) for c in range(8)]
    return np.ascontiguousarray(np.concatenate(outs, axis=0))


# version 18
# speedup vs baseline: 1.1529x; 1.0643x over previous
import numpy as np
import ml_dtypes
import concourse.bass as bass
import concourse.mybir as mybir
from concourse.bass_utils import run_bass_kernel_spmd
from contextlib import ExitStack

F32 = mybir.dt.float32
BF16 = mybir.dt.bfloat16
I32 = mybir.dt.int32
U8 = mybir.dt.uint8
AF = mybir.ActivationFunctionType
ALU = mybir.AluOpType
bfdt = ml_dtypes.bfloat16

S = 2048
E = 2048
EC = 16
CQ = 1536
CQC = 12
CKV = 512
CKVC = 4
RD = 64
H = 16
HD = 128
HID = 5632
HIDC = 44
NSEQ = 2
TT = 1024
NSUB = 2
NTILE = S // TT
NG = 512
SCALE = float(1.0 / np.sqrt(HD + RD))
ESHIFT = -12.0
EPS = 1e-5
QG = 512
NQG = S // QG
KCPG = QG // 128


class Buf:

    def __init__(self):
        self.reads = []
        self.wr = None

    def for_write(self):
        w = list(self.reads)
        if self.wr is not None:
            w.append(self.wr)
        return w

    def for_read(self):
        return [self.wr] if self.wr is not None else []

    def did_write(self, sem, val):
        self.reads = []
        self.wr = (sem, val)

    def did_read(self, sem, val):
        self.reads.append((sem, val))


class Ctx:

    ENGS = ("pe", "act", "dve", "sp", "pl")
    DMA_SEMS = ("sp", "pl", "st0", "st1", "lnb", "g0", "g1")
    ALL_SEMS = ENGS + ("st0", "st1", "lnb", "g0", "g1")

    def __init__(self, nc, es):
        self.nc = nc
        self.sem = {n: es.enter_context(nc.semaphore("sem_" + n))
                    for n in self.ALL_SEMS}
        self.cnt = {n: 0 for n in self.ALL_SEMS}
        self.lastw = {}
        self.q = {n: [] for n in self.ENGS}

    def step(self, ename, emit, reads=(), writes=(), inc=1, dsem=None):
        sname = dsem or ename
        waits = []
        for b in reads:
            waits += b.for_read()
        for b in writes:
            waits += b.for_write()
        wlist = []
        need = {}
        for dep, val in waits:
            if dep == sname:
                continue
            assert val <= self.cnt[dep], (ename, dep, val, self.cnt[dep])
            if dep in self.DMA_SEMS:
                val = self.cnt[dep]
            need[dep] = max(need.get(dep, 0), val)
        for dep, val in need.items():
            if self.lastw.get((ename, dep), -1) < val:
                self.lastw[(ename, dep)] = val
                wlist.append((dep, val))
        n = 16 if sname in self.DMA_SEMS else 1
        self.cnt[sname] += n * inc
        mark = self.cnt[sname]
        sem = self.sem[sname]

        def run(eng, wlist=wlist, emit=emit, sem=sem, n=n, inc=inc):
            for dep, val in wlist:
                eng.wait_ge(self.sem[dep], val)
            last = emit(eng)
            if inc:
                assert last is not None
                if isinstance(last, (list, tuple)):
                    assert len(last) == inc
                    for one in last:
                        one.then_inc(sem, n)
                else:
                    last.then_inc(sem, n * inc)

        self.q[ename].append(run)
        for b in reads:
            b.did_read(sname, mark)
        for b in writes:
            b.did_write(sname, mark)
        return mark

    def raw_wait(self, ename, dep=None, val=None):
        dep = dep or ename
        val = self.cnt[dep] if val is None else val

        def run(eng, dep=dep, val=val):
            eng.wait_ge(self.sem[dep], val)
        self.q[ename].append(run)
        self.lastw[(ename, dep)] = max(self.lastw.get((ename, dep), -1), val)

    SOFT = False

    def soft_barrier(self):
        if not self.SOFT:
            self.flush()
            return
        for e in self.ENGS:
            for d in self.ALL_SEMS:
                if d == e:
                    continue
                v = self.cnt[d]
                if v > 0 and self.lastw.get((e, d), -1) < v:
                    self.raw_wait(e, d, v)

    def flush(self):
        qs = self.q
        self.q = {n: [] for n in self.ENGS}
        if not any(qs.values()):
            return
        for dma in self.DMA_SEMS:
            v = self.cnt[dma]
            if v > 0:
                tgt = "pl" if qs["pl"] else "sp"

                def fin(eng, dma=dma, v=v):
                    eng.wait_ge(self.sem[dma], v)
                qs[tgt].append(fin)

        def mk(name):
            def body(eng):
                for fn in qs[name]:
                    fn(eng)
            return body

        with self.nc.Block() as blk:
            if qs["pe"]:
                blk.tensor(mk("pe"))
            if qs["act"]:
                blk.scalar(mk("act"))
            if qs["dve"]:
                blk.vector(mk("dve"))
            if qs["sp"]:
                blk.sync(mk("sp"))
            if qs["pl"]:
                blk.gpsimd(mk("pl"))
        for e in self.ENGS:
            for nm in self.ALL_SEMS:
                self.lastw[(e, nm)] = max(self.lastw.get((e, nm), -1), self.cnt[nm])



def build_core_program(dumps=(), stop_after=None, nseq=NSEQ):
    nc = bass.Bass("TRN2", target_bir_lowering=False)

    x_in = nc.dram_tensor("x_in", [nseq, E, S], F32, kind="ExternalInput").ap()
    wdqT = nc.dram_tensor("wdqT", [E, CQ], F32, kind="ExternalInput").ap()
    wuqT = nc.dram_tensor("wuqT", [CQ, E], F32, kind="ExternalInput").ap()
    wdkvT = nc.dram_tensor("wdkvT", [E, CKV], F32, kind="ExternalInput").ap()
    wukT = nc.dram_tensor("wukT", [CKV, E], F32, kind="ExternalInput").ap()
    wuvT = nc.dram_tensor("wuvT", [CKV, E], F32, kind="ExternalInput").ap()
    wqrT = nc.dram_tensor("wqrT", [CQ, H * RD], F32, kind="ExternalInput").ap()
    wkrT = nc.dram_tensor("wkrT", [E, H * RD], F32, kind="ExternalInput").ap()
    woutT = nc.dram_tensor("woutT", [E, E], F32, kind="ExternalInput").ap()
    wgT = nc.dram_tensor("wgT", [E, HID], U8, kind="ExternalInput").ap()
    wuT = nc.dram_tensor("wuT", [E, HID], U8, kind="ExternalInput").ap()
    wdT = nc.dram_tensor("wdT", [HID, E], U8, kind="ExternalInput").ap()
    sg_in = nc.dram_tensor("sg_in", [128, HIDC], F32, kind="ExternalInput").ap()
    su_in = nc.dram_tensor("su_in", [128, HIDC], F32, kind="ExternalInput").ap()
    sd_in = nc.dram_tensor("sd_in", [128, EC], F32, kind="ExternalInput").ap()
    cos_in = nc.dram_tensor("cos_in", [128, S], BF16, kind="ExternalInput").ap()
    sin_in = nc.dram_tensor("sin_in", [128, S], BF16, kind="ExternalInput").ap()
    mask_in = nc.dram_tensor("mask_in", [4, 128, NG], BF16, kind="ExternalInput").ap()
    ones_c = nc.dram_tensor("ones_c", [128, 1], BF16, kind="ExternalInput").ap()
    ones_r = nc.dram_tensor("ones_r", [1, 128], BF16, kind="ExternalInput").ap()

    y_out = nc.dram_tensor("y_out", [nseq, E, S], F32, kind="ExternalOutput").ap()

    wg_s = nc.dram_tensor("wg_s", [HIDC, 128, EC * 128], BF16, kind="Internal").ap()
    wu_s = nc.dram_tensor("wu_s", [HIDC, 128, EC * 128], BF16, kind="Internal").ap()
    wd_s = nc.dram_tensor("wd_s", [EC, 128, HIDC * 128], BF16, kind="Internal").ap()
    x1_s = nc.dram_tensor("x1_s", [E, S], F32, kind="Internal").ap()
    x2_s = nc.dram_tensor("x2_s", [E, S], F32, kind="Internal").ap()
    ao_s = nc.dram_tensor("ao_s", [H, HD, S], BF16, kind="Internal").ap()

    dump_aps = {}
    for name in dumps:
        src = {"x1": x1_s, "x2": x2_s, "ao": ao_s}[name]
        dump_aps[name] = (nc.dram_tensor("dump_" + name, list(src.shape),
                                         src.dtype, kind="ExternalOutput").ap(), src)

    es = ExitStack()
    with es:
        C = Ctx(nc, es)

        def sb(name, shape, dtype, stack=es):
            return stack.enter_context(nc.sbuf_tensor(name, shape, dtype))

        def pst(name, shape, dtype=F32, stack=es):
            return stack.enter_context(nc.psum_tensor(name, shape, dtype))

        g = dict(
            cosT=sb("cosT", [128, S], BF16), sinT=sb("sinT", [128, S], BF16),
            sg=sb("sg", [128, HIDC], F32), su=sb("su", [128, HIDC], F32),
            sdh=sb("sdh", [128, EC], F32),
            onec=sb("onec", [128, 1], BF16), oner=sb("oner", [1, 128], BF16),
            c_eps=sb("c_eps", [1, 1], F32), c_shift=sb("c_shift", [128, 1], F32),
        )
        lnA = sb("lnA", [128, EC * TT], BF16)
        lnA_b = Buf()
        g["mask_in"] = mask_in

        consts = Buf()
        for dst, src in ((g["cosT"], cos_in), (g["sinT"], sin_in),
                         (g["sg"], sg_in), (g["su"], su_in), (g["sdh"], sd_in),
                         (g["onec"], ones_c), (g["oner"], ones_r)):
            C.step("sp", lambda e, dst=dst, src=src: e.dma_start(out=dst[:], in_=src),
                   writes=[consts])
        C.step("dve", lambda e: e.tensor_scalar_mul(g["sdh"][:], g["sdh"][:], 0.5),
               reads=[consts], writes=[consts])
        C.step("dve", lambda e: e.memset(g["c_eps"][:], EPS), writes=[consts])
        C.step("dve", lambda e: e.memset(g["c_shift"][:], ESHIFT), writes=[consts])
        C.flush()

        with ExitStack() as st0:
            p0i = sb("p0_i", [128, 4 * 2048], U8, st0)
            p0b = sb("p0_b", [128, 4 * 2048], BF16, st0)
            p0is = [Buf() for _ in range(4)]
            p0bs = [Buf() for _ in range(4)]
            jobs0 = ([("wg", wgT, wg_s, oc, EC) for oc in range(HIDC)] +
                     [("wu", wuT, wu_s, oc, EC) for oc in range(HIDC)] +
                     [("wd", wdT, wd_s, oc, HIDC) for oc in range(EC)])
            jctr = 0
            for kind0, srcw0, dstw0, oc0, nic0 in jobs0:
                for c00 in range(0, nic0, 16):
                    n0 = min(16, nic0 - c00)
                    sl0 = jctr % 4
                    q0 = ("sp", "pl")[jctr % 2]
                    jctr += 1
                    X0 = n0 * 128
                    C.step(q0, lambda e, srcw0=srcw0, oc0=oc0, sl0=sl0, X0=X0, n0=n0, c00=c00:
                           e.dma_start(out=p0i.ap()[:, sl0 * 2048:sl0 * 2048 + X0].rearrange(
                                           "p (a n) -> p a n", a=n0),
                                       in_=srcw0.rearrange("(a p) m -> p a m", p=128)[
                                           :, c00:c00 + n0, oc0 * 128:(oc0 + 1) * 128]),
                           writes=[p0is[sl0]], dsem="g%d" % (sl0 % 2))
                    C.step("dve", lambda e, sl0=sl0, X0=X0:
                           e.tensor_scalar_add(p0b[:, sl0 * 2048:sl0 * 2048 + X0],
                                               p0i[:, sl0 * 2048:sl0 * 2048 + X0], -128.0),
                           reads=[p0is[sl0]], writes=[p0bs[sl0]])
                    C.step(("pl", "sp")[jctr % 2],
                           lambda e, dstw0=dstw0, oc0=oc0, sl0=sl0, X0=X0, c00=c00:
                           e.dma_start(out=dstw0[oc0][:, c00 * 128:c00 * 128 + X0],
                                       in_=p0b[:, sl0 * 2048:sl0 * 2048 + X0]),
                           reads=[p0bs[sl0]], dsem="st%d" % (sl0 % 2))
            C.soft_barrier()

        x1_db = [Buf() for _ in range(NTILE)]
        x2_db = [Buf() for _ in range(NTILE)]
        _ln_load(nc, C, lnA, lnA_b, x_in[0], 0)
        for s in range(nseq):
            _ffn_pass(nc, C, sb, pst, x_in[s], x1_s, wg_s, wu_s, wd_s, g,
                      "f1s%d" % s, lnA, lnA_b, x1_db, dq=None,
                      post_ln=(x1_s, x1_db[0]), skip_first_ln=(s > 0))
            C.soft_barrier()
            if stop_after == "P1" and s == nseq - 1:
                _finish(nc, C, dump_aps)
                return nc

            with ExitStack() as st23:
                c_q = sb("cq%d" % s, [128, CQC * S], BF16, st23)
                c_kv = sb("ckv%d" % s, [128, CKVC * S], BF16, st23)
                k_r = sb("kr%d" % s, [128, (H // 2) * S], BF16, st23)
                lat = Buf()
                _proj_pass(nc, C, sb, pst, x1_s, c_q, c_kv, k_r, lat,
                           wdqT, wdkvT, wkrT, g, "p2s%d" % s, lnA, lnA_b, x1_db)
                C.soft_barrier()
                if stop_after == "P2" and s == nseq - 1:
                    _finish(nc, C, dump_aps,
                            extra=[("c_q", c_q), ("c_kv", c_kv), ("k_r", k_r)])
                    return nc
                _attention(nc, C, sb, pst, c_q, c_kv, k_r, lat,
                           wuqT, wqrT, wukT, wuvT, g, ao_s, "p3s%d" % s)
            if stop_after == "P3" and s == nseq - 1:
                _finish(nc, C, dump_aps)
                return nc

            _outproj(nc, C, sb, pst, ao_s, woutT, x1_s, x2_s, g, "oas%d" % s,
                     x1_db, x2_db, lnA, lnA_b, post_ln=(x2_s, x2_db[0]))
            C.soft_barrier()
            if stop_after == "P4a" and s == nseq - 1:
                _finish(nc, C, dump_aps)
                return nc

            _ffn_pass(nc, C, sb, pst, x2_s, y_out[s], wg_s, wu_s, wd_s, g,
                      "f2s%d" % s, lnA, lnA_b, None, skip_first_ln=True,
                      post_ln=(x_in[s + 1], None) if s + 1 < nseq else None)
            C.soft_barrier()

        _finish(nc, C, dump_aps)
    return nc


def _finish(nc, C, dump_aps, extra=None):
    for name, (dap, src) in dump_aps.items():
        C.step("pl", lambda e, dap=dap, src=src: e.dma_start(out=dap, in_=src))
    if extra:
        for nm, t in extra:
            dap = nc.dram_tensor("dumpsb_" + nm, list(t.ap().shape),
                                 t.ap().dtype, kind="ExternalOutput").ap()
            C.step("pl", lambda e, dap=dap, t=t: e.dma_start(out=dap, in_=t[:]))
    C.raw_wait("pl")
    C.raw_wait("sp")
    C.flush()


def _ln_load(nc, C, lnb, lnb_b, src_view, t0, src_b=None):
    for sub in range(NSUB):
        sl = lnb.ap().rearrange("p (c t) -> p c t", c=EC)[:, :, sub * NG:(sub + 1) * NG]
        C.step("pl", lambda e, sl=sl, sub=sub: e.dma_start(
            out=sl,
            in_=src_view.rearrange("(c p) t -> p c t", p=128)[
                :, :, t0 + sub * NG:t0 + (sub + 1) * NG]),
            reads=([src_b] if src_b is not None else []), writes=[lnb_b],
            dsem="lnb")


def _ln_compute(nc, C, lnb, lnb_b, sqb, sqb_b, tmpb, tmpb_b, rowset, sps, spb, g):
    rb, rowbufs = rowset
    for sub in range(NSUB):
        s0 = sub * NG
        for c in range(EC):
            csl = lnb[:, c * TT + s0:c * TT + s0 + NG]
            ssl = sqb[:, (c % 2) * NG:(c % 2 + 1) * NG]
            C.step("dve", lambda e, csl=csl, ssl=ssl: e.tensor_mul(ssl, csl, csl),
                   reads=[lnb_b], writes=[sqb_b[c % 2]])

            def mm(e, c=c, csl=csl, ssl=ssl):
                e.matmul(sps[0:1, 0:NG], g["onec"][:], csl,
                         start=(c == 0), stop=(c == EC - 1))
                return e.matmul(sps[0:1, NG:2 * NG], g["onec"][:], ssl,
                                start=(c == 0), stop=(c == EC - 1))
            C.step("pe", mm, reads=[lnb_b, sqb_b[c % 2], consts_of(g)],
                   writes=[spb[0], spb[1]])
        n = 1.0 / E

        def rows(e, sub=sub):
            e.tensor_scalar_mul(rb["mean"][0:1, sub * NG:(sub + 1) * NG],
                                sps[0:1, 0:NG], n)
            e.tensor_scalar_mul(rb["ex2"][0:1, sub * NG:(sub + 1) * NG],
                                sps[0:1, NG:2 * NG], n)
            e.tensor_mul(rb["msq"][0:1, sub * NG:(sub + 1) * NG],
                         rb["mean"][0:1, sub * NG:(sub + 1) * NG],
                         rb["mean"][0:1, sub * NG:(sub + 1) * NG])
            return e.tensor_sub(rb["var"][0:1, sub * NG:(sub + 1) * NG],
                                rb["ex2"][0:1, sub * NG:(sub + 1) * NG],
                                rb["msq"][0:1, sub * NG:(sub + 1) * NG])
        C.step("dve", rows, reads=[spb[0], spb[1]], writes=[rowbufs])

    def rstd(e):
        e.activation(rb["lnv"][:], rb["var"][:], AF.Ln, bias=g["c_eps"][:])
        return e.activation(rb["rstd"][:], rb["lnv"][:], AF.Exp, scale=-0.5)
    C.step("act", rstd, reads=[rowbufs, consts_of(g)], writes=[rowbufs])

    def tobf(e):
        e.tensor_copy(rb["mean_b"][:], rb["mean"][:])
        return e.tensor_copy(rb["rstd_b"][:], rb["rstd"][:])
    C.step("dve", tobf, reads=[rowbufs], writes=[rowbufs])

    for sub in range(NSUB):
        s0 = sub * NG

        def bcast(e, sub=sub):
            e.matmul(sps[:, 0:NG], g["oner"][:],
                     rb["mean_b"][0:1, sub * NG:(sub + 1) * NG],
                     start=True, stop=True)
            return e.matmul(sps[:, NG:2 * NG], g["oner"][:],
                            rb["rstd_b"][0:1, sub * NG:(sub + 1) * NG],
                            start=True, stop=True)
        C.step("pe", bcast, reads=[rowbufs, consts_of(g)], writes=[spb[0], spb[1]])

        for c in range(EC):
            csl = lnb[:, c * TT + s0:c * TT + s0 + NG]
            tsl = tmpb[:, (c % 2) * NG:(c % 2 + 1) * NG]

            def norm(e, csl=csl, tsl=tsl):
                e.tensor_tensor(tsl, csl, sps[:, 0:NG], ALU.subtract)
                return e.tensor_tensor(csl, tsl, sps[:, NG:2 * NG], ALU.mult)
            C.step("dve", norm, reads=[spb[0], spb[1], lnb_b],
                   writes=[lnb_b, tmpb_b[c % 2]])


_CONSTS_BUF = {}


def consts_of(g):
    key = id(g)
    if key not in _CONSTS_BUF:
        _CONSTS_BUF[key] = Buf()
    return _CONSTS_BUF[key]


def _mk_rowset(sb, stack, tag):
    W = NSUB * NG
    R = sb(tag + "_rows", [1, 3 * W], F32, stack)
    B = sb(tag + "_rowsb", [1, 2 * W], BF16, stack)
    rb = dict(
        mean=R[0:1, 0:W], ex2=R[0:1, W:2 * W], msq=R[0:1, 2 * W:3 * W],
        var=R[0:1, W:2 * W], lnv=R[0:1, 2 * W:3 * W], rstd=R[0:1, W:2 * W],
        mean_b=B[0:1, 0:W], rstd_b=B[0:1, W:2 * W],
    )
    return rb, Buf()


def _ffn_pass(nc, C, sb, pst, src_dram, dst_dram, wg_s, wu_s, wd_s, g, tag,
              lnA, lnA_b, dst_db, dq=None, post_ln=None, skip_first_ln=False):
    with ExitStack() as stF:
        lnb, lnb_b = lnA, lnA_b
        hbuf = sb(tag + "_h", [128, HIDC * TT], BF16, stF)
        wgbuf = sb(tag + "_wg", [128, 2 * 2048], BF16, stF)
        wubuf = sb(tag + "_wu", [128, 2 * 2048], BF16, stF)
        wdbuf = sb(tag + "_wd", [128, 2 * HID], BF16, stF)
        gbuf = sb(tag + "_g", [128, 2 * NG], BF16, stF)
        sqb = sb(tag + "_sq", [128, 2 * NG], BF16, stF)
        tmpb = sb(tag + "_tmp", [128, 2 * NG], BF16, stF)
        xep = sb(tag + "_xep", [128, 2 * NG], F32, stF)
        x1o = sb(tag + "_x1o", [128, 2 * NG], F32, stF)
        rowset = _mk_rowset(sb, stF, tag)
        if dq:
            wgT_, wuT_, wdT_ = dq
            ibuf = sb(tag + "_ib", [128, 2 * 1024], U8, stF)
            isl = [Buf(), Buf()]
            wdst = sb(tag + "_wds", [128, 1024], BF16, stF)
            wdsl_st = [Buf()]

        h_b = [Buf() for _ in range(HIDC * 2)]
        wgsl = [Buf(), Buf()]
        wusl = [Buf(), Buf()]
        wdsl = [Buf(), Buf()]
        gsl = [Buf(), Buf()]
        sqsl = [Buf(), Buf()]
        tmsl = [Buf(), Buf()]
        xsl = [Buf(), Buf()]
        osl = [Buf(), Buf()]

        gate_ps = pst(tag + "_gp", [128, TT], F32, stF)
        up_ps = pst(tag + "_up", [128, TT], F32, stF)
        down_ps = pst(tag + "_dp", [128, TT], F32, stF)
        gate_pb = [Buf(), Buf()]
        up_pb = [Buf(), Buf()]
        down_pb = [Buf(), Buf()]

        if not skip_first_ln:
            _ln_compute(nc, C, lnb, lnb_b, sqb, sqsl, tmpb, tmsl, rowset,
                        gate_ps, gate_pb, g)

        wdram = {}
        wd_done = [0]
        dqctr = [0]

        def dq_chain(kind, srcw, scratch, oc, wbuf_, wsl_, sl, nic, c0=0):
            X = nic * 128
            W = 2048 if kind != "wd" else HID
            il = dqctr[0] % 2
            dqctr[0] += 1
            C.step("sp", lambda e: e.dma_start(
                out=ibuf.ap()[:, il * 1024:il * 1024 + X].rearrange(
                    "p (a n) -> p a n", a=nic),
                in_=srcw.rearrange("(a p) m -> p a m", p=128)[
                    :, c0:c0 + nic, oc * 128:(oc + 1) * 128]),
                writes=[isl[il]], dsem="g%d" % il)
            if wbuf_ is None:
                st = 0

                C.step("dve", lambda e: e.tensor_scalar_add(
                    wdst[:, 0:X],
                    ibuf[:, il * 1024:il * 1024 + X], -128.0),
                    reads=[isl[il]], writes=[wdsl_st[0]])
                db = wdram.setdefault((kind, oc), Buf())
                C.step("pl", lambda e: e.dma_start(
                    out=scratch[oc][:, c0 * 128:c0 * 128 + X],
                    in_=wdst[:, 0:X]),
                    reads=[wdsl_st[0]], writes=[db], dsem="st0")
                return
            C.step("dve", lambda e: e.tensor_scalar_add(
                wbuf_[:, sl * W + c0 * 128:sl * W + c0 * 128 + X],
                ibuf[:, il * 1024:il * 1024 + X], -128.0),
                reads=[isl[il]], writes=[wsl_])
            db = wdram.setdefault((kind, oc), Buf())
            C.step("pl", lambda e: e.dma_start(
                out=scratch[oc][:, c0 * 128:c0 * 128 + X],
                in_=wbuf_[:, sl * W + c0 * 128:sl * W + c0 * 128 + X]),
                reads=[wsl_], writes=[db], dsem="st%d" % sl)

        for tile in range(NTILE):
            t0 = tile * TT
            for oc in range(HIDC):
                sl = oc % 2

                if dq and tile == 0:
                    dq_chain("wg", wgT_, wg_s, oc, wgbuf, wgsl[sl], sl, 8, 0)
                    dq_chain("wg", wgT_, wg_s, oc, wgbuf, wgsl[sl], sl, 8, 8)
                    dq_chain("wu", wuT_, wu_s, oc, wubuf, wusl[sl], sl, 8, 0)
                    dq_chain("wu", wuT_, wu_s, oc, wubuf, wusl[sl], sl, 8, 8)
                    wd_subs = [(og, c0, nic) for og in range(EC)
                               for c0, nic in ((0, 8), (8, 8), (16, 8),
                                               (24, 8), (32, 8), (40, 4))]
                    while wd_done[0] < (oc + 1) * len(wd_subs) // HIDC:
                        og, c0, nic = wd_subs[wd_done[0]]
                        dq_chain("wd", wdT_, wd_s, og, None, None, 0, nic, c0)
                        wd_done[0] += 1
                else:
                    def wload(e, oc=oc, sl=sl):
                        a = e.dma_start(out=wgbuf[:, sl * 2048:(sl + 1) * 2048],
                                        in_=wg_s[oc])
                        b = e.dma_start(out=wubuf[:, sl * 2048:(sl + 1) * 2048],
                                        in_=wu_s[oc])
                        return [a, b]
                    rds = [wdram[k] for k in (("wg", oc), ("wu", oc)) if k in wdram]
                    C.step("sp", wload, reads=rds,
                           writes=[wgsl[sl], wusl[sl]], inc=2)

                for ng in range(2):
                    def gmm(e, oc=oc, sl=sl, ng=ng):
                        for ic in range(EC):
                            ins = e.matmul(
                                gate_ps[:, ng * NG:(ng + 1) * NG],
                                wgbuf[:, sl * 2048 + ic * 128:sl * 2048 + (ic + 1) * 128],
                                lnb[:, ic * TT + ng * NG:ic * TT + (ng + 1) * NG],
                                start=(ic == 0), stop=(ic == EC - 1))
                        return ins
                    C.step("pe", gmm, reads=[wgsl[sl], lnb_b], writes=[gate_pb[ng]])

                    def umm(e, oc=oc, sl=sl, ng=ng):
                        for ic in range(EC):
                            ins = e.matmul(
                                up_ps[:, ng * NG:(ng + 1) * NG],
                                wubuf[:, sl * 2048 + ic * 128:sl * 2048 + (ic + 1) * 128],
                                lnb[:, ic * TT + ng * NG:ic * TT + (ng + 1) * NG],
                                start=(ic == 0), stop=(ic == EC - 1))
                        return ins
                    C.step("pe", umm, reads=[wusl[sl], lnb_b], writes=[up_pb[ng]])

                    C.step("act", lambda e, oc=oc, ng=ng: e.activation(
                        gbuf[:, ng * NG:(ng + 1) * NG],
                        gate_ps[:, ng * NG:(ng + 1) * NG],
                        AF.Silu, scale=g["sg"][:, oc:oc + 1]),
                        reads=[gate_pb[ng], consts_of(g)], writes=[gsl[ng]])

                    C.step("dve", lambda e, oc=oc, ng=ng: e.scalar_tensor_tensor(
                        hbuf[:, oc * TT + ng * NG:oc * TT + (ng + 1) * NG],
                        up_ps[:, ng * NG:(ng + 1) * NG],
                        g["su"][:, oc:oc + 1],
                        gbuf[:, ng * NG:(ng + 1) * NG],
                        ALU.mult, ALU.mult),
                        reads=[up_pb[ng], gsl[ng], consts_of(g)],
                        writes=[h_b[oc * 2 + ng]])

            if tile + 1 < NTILE:
                _ln_load(nc, C, lnb, lnb_b, src_dram, t0 + TT)

            for og in range(EC):
                sl = og % 2
                rds = [wdram[k] for k in (("wd", og),) if k in wdram]
                C.step("sp", lambda e, og=og, sl=sl: e.dma_start(
                    out=wdbuf[:, sl * HID:(sl + 1) * HID], in_=wd_s[og]),
                    reads=rds, writes=[wdsl[sl]])
                for ng in range(2):
                    xslot = ng
                    C.step("sp", lambda e, og=og, ng=ng, xslot=xslot, t0=t0:
                           e.dma_start(
                               out=xep[:, xslot * NG:(xslot + 1) * NG],
                               in_=src_dram[og * 128:(og + 1) * 128,
                                            t0 + ng * NG:t0 + (ng + 1) * NG]),
                           writes=[xsl[xslot]])

                    def dmms(e, og=og, sl=sl, ng=ng):
                        for hc in range(HIDC):
                            ins = e.matmul(
                                down_ps[:, ng * NG:(ng + 1) * NG],
                                wdbuf[:, sl * HID + hc * 128:sl * HID + (hc + 1) * 128],
                                hbuf[:, hc * TT + ng * NG:hc * TT + (ng + 1) * NG],
                                start=(hc == 0), stop=(hc == HIDC - 1))
                        return ins
                    C.step("pe", dmms,
                           reads=[wdsl[sl]] + [h_b[hc * 2 + ng] for hc in range(HIDC)],
                           writes=[down_pb[ng]])

                    C.step("dve", lambda e, og=og, ng=ng, xslot=xslot:
                           e.scalar_tensor_tensor(
                               x1o[:, xslot * NG:(xslot + 1) * NG],
                               down_ps[:, ng * NG:(ng + 1) * NG],
                               g["sdh"][:, og:og + 1],
                               xep[:, xslot * NG:(xslot + 1) * NG],
                               ALU.mult, ALU.add),
                           reads=[down_pb[ng], xsl[xslot], consts_of(g)],
                           writes=[osl[xslot]])

                    C.step("pl", lambda e, og=og, ng=ng, xslot=xslot, t0=t0:
                           e.dma_start(
                               out=dst_dram[og * 128:(og + 1) * 128,
                                            t0 + ng * NG:t0 + (ng + 1) * NG],
                               in_=x1o[:, xslot * NG:(xslot + 1) * NG]),
                           reads=[osl[xslot]],
                           writes=([dst_db[tile]] if dst_db else []),
                           dsem="st%d" % ng)
                if og == 7 and tile + 1 < NTILE:
                    _ln_compute(nc, C, lnb, lnb_b, sqb, sqsl, tmpb, tmsl,
                                rowset, gate_ps, gate_pb, g)
        if post_ln is not None:
            nsrc, nsrc_b = post_ln
            _ln_load(nc, C, lnb, lnb_b, nsrc, 0, src_b=nsrc_b)
            _ln_compute(nc, C, lnb, lnb_b, sqb, sqsl, tmpb, tmsl, rowset,
                        gate_ps, gate_pb, g)


def _proj_pass(nc, C, sb, pst, x1_s, c_q, c_kv, k_r, lat,
               wdqT, wdkvT, wkrT, g, tag, lnA, lnA_b, x1_db):
    with ExitStack() as stP:
        lnbs = [lnA, sb(tag + "_ln1", [128, EC * TT], BF16, stP)]
        wbuf = sb(tag + "_w", [128, 2 * 2048], BF16, stP)
        sqb = sb(tag + "_sq", [128, 2 * NG], BF16, stP)
        tmpb = sb(tag + "_tmp", [128, 2 * NG], BF16, stP)
        rop1 = sb(tag + "_r1", [128, TT], F32, stP)
        rop2 = sb(tag + "_r2", [128, TT], F32, stP)
        rowset = _mk_rowset(sb, stP, tag)
        lnb_bs = [lnA_b, Buf()]
        wsl = [Buf(), Buf()]
        sqsl = [Buf(), Buf()]
        tmsl = [Buf(), Buf()]
        ropb = Buf()

        pj_ps = pst(tag + "_pj", [128, 2 * NG], F32, stP)
        kr_ps = pst(tag + "_kr", [128, TT], F32, stP)
        pj_pb = [Buf(), Buf()]
        kr_pb = Buf()


        for tile in range(NTILE):
            t0 = tile * TT
            lnb = lnbs[tile % 2]
            lnb_b = lnb_bs[tile % 2]
            if tile + 1 < NTILE:
                _ln_load(nc, C, lnbs[(tile + 1) % 2], lnb_bs[(tile + 1) % 2],
                         x1_s, t0 + TT, src_b=x1_db[tile + 1])
            jobs = ([("cq", wdqT, oc, c_q) for oc in range(CQC)] +
                    [("ckv", wdkvT, oc, c_kv) for oc in range(CKVC)] +
                    [("kr", wkrT, oc, k_r) for oc in range(H // 2)])
            for j, (kind, wsrc, oc, dbuf) in enumerate(jobs):
                sl = j % 2
                C.step("pl", lambda e, wsrc=wsrc, oc=oc, sl=sl: e.dma_start(
                    out=wbuf.ap()[:, sl * 2048:(sl + 1) * 2048].rearrange(
                        "p (a n) -> p a n", a=EC),
                    in_=wsrc.rearrange("(a p) m -> p a m", p=128)[
                        :, :, oc * 128:(oc + 1) * 128]),
                    writes=[wsl[sl]], dsem="g%d" % sl)

                if kind != "kr":
                    for ng in range(2):
                        def pmm(e, sl=sl, ng=ng, lnb=lnb):
                            for ic in range(EC):
                                ins = e.matmul(
                                    pj_ps[:, ng * NG:(ng + 1) * NG],
                                    wbuf[:, sl * 2048 + ic * 128:sl * 2048 + (ic + 1) * 128],
                                    lnb[:, ic * TT + ng * NG:ic * TT + (ng + 1) * NG],
                                    start=(ic == 0), stop=(ic == EC - 1))
                            return ins
                        C.step("pe", pmm, reads=[wsl[sl], lnb_b], writes=[pj_pb[ng]])
                        C.step("dve", lambda e, oc=oc, ng=ng, dbuf=dbuf, t0=t0:
                               e.tensor_copy(
                                   dbuf[:, oc * S + t0 + ng * NG:oc * S + t0 + (ng + 1) * NG],
                                   pj_ps[:, ng * NG:(ng + 1) * NG]),
                               reads=[pj_pb[ng]], writes=[lat])
                else:
                    for ng in range(2):
                        def kmm(e, sl=sl, ng=ng, lnb=lnb):
                            for ic in range(EC):
                                ins = e.matmul(
                                    kr_ps[:, ng * NG:(ng + 1) * NG],
                                    wbuf[:, sl * 2048 + ic * 128:sl * 2048 + (ic + 1) * 128],
                                    lnb[:, ic * TT + ng * NG:ic * TT + (ng + 1) * NG],
                                    start=(ic == 0), stop=(ic == EC - 1))
                            return ins
                        C.step("pe", kmm, reads=[wsl[sl], lnb_b], writes=[kr_pb])

                    def rope(e, oc=oc, dbuf=dbuf, t0=t0):
                        e.tensor_tensor(rop1[:, 0:TT], kr_ps[:, 0:TT],
                                        g["cosT"][:, t0:t0 + TT], ALU.mult)
                        for g4 in range(4):
                            src0 = (g4 + 1) * 32 if g4 % 2 == 0 else (g4 - 1) * 32
                            e.tensor_tensor(
                                rop2[g4 * 32:(g4 + 1) * 32, 0:TT],
                                kr_ps[src0:src0 + 32, 0:TT],
                                g["sinT"][g4 * 32:(g4 + 1) * 32, t0:t0 + TT],
                                ALU.mult)
                        return e.tensor_add(
                            dbuf[:, oc * S + t0:oc * S + t0 + TT],
                            rop1[:, 0:TT], rop2[:, 0:TT])
                    C.step("dve", rope, reads=[kr_pb, consts_of(g)],
                           writes=[lat, ropb, kr_pb])
                if j == CQC + CKVC - 1 and tile + 1 < NTILE:
                    _ln_compute(nc, C, lnbs[(tile + 1) % 2],
                                lnb_bs[(tile + 1) % 2], sqb, sqsl, tmpb, tmsl,
                                rowset, pj_ps, pj_pb, g)


def _attention(nc, C, sb, pst, c_q, c_kv, k_r, lat,
               wuqT, wqrT, wukT, wuvT, g, ao_s, tag):
    with ExitStack() as stA:
        kcT = sb(tag + "_kc", [128, S], BF16, stA)
        vkm = sb(tag + "_v", [128, S], BF16, stA)
        qcT = sb(tag + "_qc", [128, S], BF16, stA)
        qrT = sb(tag + "_qr", [128, S], BF16, stA)
        ebuf = sb(tag + "_e", [128, (S // 128) * NG], BF16, stA)
        wuq_b = sb(tag + "_wuq", [128, 2 * CQC * 128], BF16, stA)
        wqr_b = sb(tag + "_wqr", [128, 2 * CQC * 64], BF16, stA)
        wuk_b = sb(tag + "_wuk", [128, 2 * CKVC * 128], BF16, stA)
        wuv_b = sb(tag + "_wuv", [128, 2 * CKVC * 128], BF16, stA)
        rop1 = sb(tag + "_r1", [64, NG], F32, stA)
        rop2 = sb(tag + "_r2", [64, NG], F32, stA)
        masks = sb(tag + "_mask", [128, 4 * NG], BF16, stA)
        masks_b = Buf()
        C.step("pl", lambda e: e.dma_start(
            out=masks.ap().rearrange("p (j n) -> p j n", j=4),
            in_=g["mask_in"].rearrange("j p n -> p j n")), writes=[masks_b])
        lnz = sb(tag + "_lnz", [1, NG], F32, stA)
        recr = sb(tag + "_recr", [1, NG], F32, stA)
        recb = sb(tag + "_recb", [1, 2 * NG], BF16, stA)
        recf = sb(tag + "_recf", [128, 2 * NG], BF16, stA)
        aout = sb(tag + "_ao", [128, 2 * NG], BF16, stA)

        wq_sl = [Buf(), Buf()]
        kc_b = Buf()
        v_b = Buf()
        qc_b = Buf()
        qr_b = Buf()
        ropb = Buf()
        e_b = [Buf() for _ in range(S // 128)]
        rec_b = Buf()
        recb_b = [Buf(), Buf()]
        recf_b = [Buf(), Buf()]
        ao_b = [Buf(), Buf()]

        with ExitStack() as stP:
            sc_ps = [pst(tag + "_sc0", [128, NG], F32, stP),
                     pst(tag + "_sc1", [128, NG], F32, stP)]
            pv_ps = [pst(tag + "_pv0", [128, NG], F32, stP),
                     pst(tag + "_pv1", [128, NG], F32, stP)]
            z_ps = [pst(tag + "_z0", [1, NG], F32, stP)]
            bc_ps = pst(tag + "_bc", [128, NG], F32, stP)
            qr_ps = [pst(tag + "_qr0", [64, NG], F32, stP),
                     pst(tag + "_qr1", [64, NG], F32, stP)]
            sc_pb = [Buf(), Buf()]
            pv_pb = [Buf(), Buf()]
            z_pb = [Buf()]
            bc_pb = Buf()
            qr_pb = [Buf(), Buf()]

            pend = []

            def emit_tail(h, qg, qp):
                C.step("pe", lambda e, qp=qp: e.matmul(
                    bc_ps[:], g["oner"][:], recb[0:1, qp * NG:(qp + 1) * NG],
                    start=True, stop=True),
                    reads=[recb_b[qp], consts_of(g)], writes=[bc_pb])
                C.step("dve", lambda e, qp=qp: e.tensor_copy(
                    recf[:, qp * NG:(qp + 1) * NG], bc_ps[:]),
                    reads=[bc_pb], writes=[recf_b[qp]])
                C.step("dve", lambda e, qp=qp: e.tensor_tensor(
                    aout[:, qp * NG:(qp + 1) * NG], pv_ps[qp][:],
                    recf[:, qp * NG:(qp + 1) * NG], ALU.mult),
                    reads=[pv_pb[qp], recf_b[qp]], writes=[ao_b[qp]])
                C.step("pl", lambda e, h=h, qg=qg, qp=qp: e.dma_start(
                    out=ao_s[h][:, qg * NG:(qg + 1) * NG],
                    in_=aout[:, qp * NG:(qp + 1) * NG]),
                    reads=[ao_b[qp]], dsem="st%d" % qp)

            for h in range(H):
                sl = h % 2
                krbase = 64 * (h % 2)
                krcol = (h // 2) * S

                def wload(e, h=h, sl=sl):
                    r = []
                    r.append(e.dma_start(
                        out=wuq_b.ap()[:, sl * CQC * 128:(sl + 1) * CQC * 128].rearrange(
                            "p (a n) -> p a n", a=CQC),
                        in_=wuqT.rearrange("(a p) m -> p a m", p=128)[
                            :, :, h * 128:(h + 1) * 128]))
                    r.append(e.dma_start(
                        out=wqr_b.ap()[:, sl * CQC * 64:(sl + 1) * CQC * 64].rearrange(
                            "p (a n) -> p a n", a=CQC),
                        in_=wqrT.rearrange("(a p) m -> p a m", p=128)[
                            :, :, h * 64:(h + 1) * 64]))
                    r.append(e.dma_start(
                        out=wuk_b.ap()[:, sl * CKVC * 128:(sl + 1) * CKVC * 128].rearrange(
                            "p (a n) -> p a n", a=CKVC),
                        in_=wukT.rearrange("(a p) m -> p a m", p=128)[
                            :, :, h * 128:(h + 1) * 128]))
                    r.append(e.dma_start(
                        out=wuv_b.ap()[:, sl * CKVC * 128:(sl + 1) * CKVC * 128].rearrange(
                            "p (a n) -> p a n", a=CKVC),
                        in_=wuvT.rearrange("(a p) m -> p a m", p=128)[
                            :, :, h * 128:(h + 1) * 128]))
                    return r
                C.step("pl", wload, writes=[wq_sl[sl]], inc=4, dsem="g%d" % sl)

                for ng in range(NQG):
                    pp = ng % 2

                    def kcmm(e, sl=sl, ng=ng, pp=pp):
                        for ic in range(CKVC):
                            ins = e.matmul(
                                sc_ps[pp][:],
                                wuk_b[:, sl * CKVC * 128 + ic * 128:sl * CKVC * 128 + (ic + 1) * 128],
                                c_kv[:, ic * S + ng * NG:ic * S + (ng + 1) * NG],
                                start=(ic == 0), stop=(ic == CKVC - 1))
                        return ins
                    C.step("pe", kcmm, reads=[wq_sl[sl], lat], writes=[sc_pb[pp]])
                    C.step("dve", lambda e, ng=ng, pp=pp: e.tensor_copy(
                        kcT[:, ng * NG:(ng + 1) * NG], sc_ps[pp][:]),
                        reads=[sc_pb[pp]], writes=[kc_b])
                for ng in range(NQG):
                    pp = ng % 2

                    def vmm(e, sl=sl, ng=ng, pp=pp):
                        for tc4 in range(4):
                            tc = ng * 4 + tc4
                            for ic in range(CKVC):
                                ins = e.matmul(
                                    sc_ps[pp][:, tc4 * 128:(tc4 + 1) * 128],
                                    c_kv[:, ic * S + tc * 128:ic * S + (tc + 1) * 128],
                                    wuv_b[:, sl * CKVC * 128 + ic * 128:sl * CKVC * 128 + (ic + 1) * 128],
                                    start=(ic == 0), stop=(ic == CKVC - 1))
                        return ins
                    C.step("pe", vmm, reads=[wq_sl[sl], lat], writes=[sc_pb[pp]])
                    C.step("dve", lambda e, ng=ng, pp=pp: e.tensor_copy(
                        vkm[:, ng * NG:(ng + 1) * NG], sc_ps[pp][:]),
                        reads=[sc_pb[pp]], writes=[v_b])
                for ng in range(NQG):
                    pp = ng % 2

                    def qcmm(e, sl=sl, ng=ng, pp=pp):
                        for ic in range(CQC):
                            ins = e.matmul(
                                sc_ps[pp][:],
                                wuq_b[:, sl * CQC * 128 + ic * 128:sl * CQC * 128 + (ic + 1) * 128],
                                c_q[:, ic * S + ng * NG:ic * S + (ng + 1) * NG],
                                start=(ic == 0), stop=(ic == CQC - 1))
                        return ins
                    C.step("pe", qcmm, reads=[wq_sl[sl], lat], writes=[sc_pb[pp]])
                    C.step("dve", lambda e, ng=ng, pp=pp: e.tensor_copy(
                        qcT[:, ng * NG:(ng + 1) * NG], sc_ps[pp][:]),
                        reads=[sc_pb[pp]], writes=[qc_b])

                    def qrmm(e, sl=sl, ng=ng, pp=pp):
                        for ic in range(CQC):
                            ins = e.matmul(
                                qr_ps[pp][:],
                                wqr_b[:, sl * CQC * 64 + ic * 64:sl * CQC * 64 + (ic + 1) * 64],
                                c_q[:, ic * S + ng * NG:ic * S + (ng + 1) * NG],
                                start=(ic == 0), stop=(ic == CQC - 1))
                        return ins
                    C.step("pe", qrmm, reads=[wq_sl[sl], lat], writes=[qr_pb[pp]])

                    def qrope(e, ng=ng, pp=pp, krbase=krbase):
                        e.tensor_tensor(rop1[:, 0:NG], qr_ps[pp][:],
                                        g["cosT"][0:64, ng * NG:(ng + 1) * NG], ALU.mult)
                        for g2 in range(2):
                            src0 = 32 if g2 == 0 else 0
                            e.tensor_tensor(
                                rop2[g2 * 32:(g2 + 1) * 32, 0:NG],
                                qr_ps[pp][src0:src0 + 32, :],
                                g["sinT"][g2 * 32:(g2 + 1) * 32, ng * NG:(ng + 1) * NG],
                                ALU.mult)
                        return e.tensor_add(qrT[krbase:krbase + 64, ng * NG:(ng + 1) * NG],
                                            rop1[:, 0:NG], rop2[:, 0:NG])
                    C.step("dve", qrope, reads=[qr_pb[pp], consts_of(g)],
                           writes=[qr_b, ropb, qr_pb[pp]])

                for qg in range(NQG):
                    nkc = (qg + 1) * KCPG
                    order = list(range(nkc - KCPG, nkc)) + list(range(nkc - KCPG))
                    for kc in order:
                        pp = kc % 2

                        def smm(e, kc=kc, pp=pp, qg=qg, krbase=krbase, krcol=krcol):
                            e.matmul(
                                sc_ps[pp][:],
                                k_r[krbase:krbase + 64,
                                    krcol + kc * 128:krcol + (kc + 1) * 128],
                                qrT[krbase:krbase + 64, qg * NG:(qg + 1) * NG],
                                start=True, stop=False)
                            return e.matmul(sc_ps[pp][:],
                                            kcT[:, kc * 128:(kc + 1) * 128],
                                            qcT[:, qg * NG:(qg + 1) * NG],
                                            start=False, stop=True)
                        C.step("pe", smm, reads=[kc_b, qc_b, qr_b, lat],
                               writes=[sc_pb[pp]])

                        C.step("act", lambda e, kc=kc, pp=pp: e.activation(
                            ebuf[:, kc * NG:(kc + 1) * NG], sc_ps[pp][:],
                            AF.Exp, bias=g["c_shift"][:], scale=SCALE),
                            reads=[sc_pb[pp], consts_of(g)], writes=[e_b[kc]])
                        if kc >= nkc - KCPG:
                            jj = kc - (nkc - KCPG)
                            C.step("dve", lambda e, kc=kc, jj=jj: e.tensor_tensor(
                                ebuf[:, kc * NG:(kc + 1) * NG],
                                ebuf[:, kc * NG:(kc + 1) * NG],
                                masks[:, jj * NG:(jj + 1) * NG], ALU.mult),
                                reads=[e_b[kc], masks_b], writes=[e_b[kc]])

                    qp = qg % 2

                    def zmm(e, qg=qg, nkc=nkc):
                        for kc in range(nkc):
                            ins = e.matmul(z_ps[0][:], g["onec"][:],
                                           ebuf[:, kc * NG:(kc + 1) * NG],
                                           start=(kc == 0), stop=(kc == nkc - 1))
                        return ins
                    C.step("pe", zmm,
                           reads=[consts_of(g)] + [e_b[kc] for kc in range(nkc)],
                           writes=[z_pb[0]])

                    def pvmm(e, qg=qg, nkc=nkc, qp=qp):
                        for kc in range(nkc):
                            ins = e.matmul(pv_ps[qp][:],
                                           vkm[:, kc * 128:(kc + 1) * 128],
                                           ebuf[:, kc * NG:(kc + 1) * NG],
                                           start=(kc == 0), stop=(kc == nkc - 1))
                        return ins
                    C.step("pe", pvmm, reads=[v_b] + [e_b[kc] for kc in range(nkc)],
                           writes=[pv_pb[qp]])

                    def recip(e, qp=qp):
                        e.activation(lnz[:], z_ps[0][:], AF.Ln)
                        return e.activation(recr[:], lnz[:], AF.Exp, scale=-1.0)
                    C.step("act", recip, reads=[z_pb[0]], writes=[rec_b])
                    C.step("dve", lambda e, qp=qp: e.tensor_copy(
                        recb[0:1, qp * NG:(qp + 1) * NG], recr[:]),
                        reads=[rec_b], writes=[recb_b[qp]])

                    if pend:
                        emit_tail(*pend.pop(0))
                    pend.append((h, qg, qp))
            while pend:
                emit_tail(*pend.pop(0))
            C.soft_barrier()


def _outproj(nc, C, sb, pst, ao_s, woutT, x1_s, x2_s, g, tag, x1_db, x2_db,
             lnA=None, lnA_b=None, post_ln=None):
    with ExitStack() as stO:
        aob = sb(tag + "_ao", [128, H * S], BF16, stO)
        wob = sb(tag + "_w", [128, 2 * 2048], BF16, stO)
        xep = sb(tag + "_x", [128, 4 * NG], F32, stO)
        x2o = sb(tag + "_o", [128, 4 * NG], F32, stO)
        sqb = sb(tag + "_sq", [128, 2 * NG], BF16, stO)
        tmpb = sb(tag + "_tmp", [128, 2 * NG], BF16, stO)
        rowset = _mk_rowset(sb, stO, tag)
        sqsl = [Buf(), Buf()]
        tmsl = [Buf(), Buf()]
        aob_b = Buf()
        wsl = [Buf(), Buf()]
        xsl = [Buf() for _ in range(4)]
        osl = [Buf() for _ in range(4)]
        with ExitStack() as stP:
            o_ps = pst(tag + "_ps", [128, 4 * NG], F32, stP)
            o_pb = [Buf() for _ in range(4)]

            C.step("sp", lambda e: e.dma_start(
                out=aob.ap().rearrange("p (a n) -> p a n", a=H)[:, 0:H // 2, :],
                in_=ao_s.rearrange("h p n -> p h n")[:, 0:H // 2, :]),
                writes=[aob_b])
            C.step("pl", lambda e: e.dma_start(
                out=aob.ap().rearrange("p (a n) -> p a n", a=H)[:, H // 2:H, :],
                in_=ao_s.rearrange("h p n -> p h n")[:, H // 2:H, :]),
                writes=[aob_b])

            for og in range(EC):
                sl = og % 2
                C.step("pl", lambda e, og=og, sl=sl: e.dma_start(
                    out=wob.ap()[:, sl * 2048:(sl + 1) * 2048].rearrange(
                        "p (a n) -> p a n", a=EC),
                    in_=woutT.rearrange("(a p) m -> p a m", p=128)[
                        :, :, og * 128:(og + 1) * 128]),
                    writes=[wsl[sl]], dsem="g%d" % sl)
                for ng in range(NQG):
                    C.step("sp", lambda e, og=og, ng=ng: e.dma_start(
                        out=xep[:, ng * NG:(ng + 1) * NG],
                        in_=x1_s[og * 128:(og + 1) * 128, ng * NG:(ng + 1) * NG]),
                        writes=[xsl[ng]])

                    def omm(e, og=og, sl=sl, ng=ng):
                        for ic in range(H):
                            ins = e.matmul(
                                o_ps[:, ng * NG:(ng + 1) * NG],
                                wob[:, sl * 2048 + ic * 128:sl * 2048 + (ic + 1) * 128],
                                aob[:, ic * S + ng * NG:ic * S + (ng + 1) * NG],
                                start=(ic == 0), stop=(ic == H - 1))
                        return ins
                    C.step("pe", omm, reads=[wsl[sl], aob_b], writes=[o_pb[ng]])

                    C.step("dve", lambda e, og=og, ng=ng: e.tensor_tensor(
                        x2o[:, ng * NG:(ng + 1) * NG],
                        o_ps[:, ng * NG:(ng + 1) * NG],
                        xep[:, ng * NG:(ng + 1) * NG], ALU.add),
                        reads=[o_pb[ng], xsl[ng]], writes=[osl[ng]])

                    C.step("pl", lambda e, og=og, ng=ng: e.dma_start(
                        out=x2_s[og * 128:(og + 1) * 128, ng * NG:(ng + 1) * NG],
                        in_=x2o[:, ng * NG:(ng + 1) * NG]),
                        reads=[osl[ng]], writes=[x2_db[ng // 2]],
                        dsem="st%d" % (ng % 2))
            if post_ln is not None:
                nsrc, nsrc_b = post_ln
                _ln_load(nc, C, lnA, lnA_b, nsrc, 0, src_b=nsrc_b)
                _ln_compute(nc, C, lnA, lnA_b, sqb, sqsl, tmpb, tmsl, rowset,
                            o_ps[:, 0:2 * NG], [o_pb[0], o_pb[1]], g)



def _host_constants():
    t = np.arange(S, dtype=np.float64)
    inv = 1.0 / (10000.0 ** (np.arange(0, RD, 2, dtype=np.float64) / RD))
    fr = np.outer(inv, t)
    cos32 = np.cos(fr).astype(np.float32)
    sin32 = np.sin(fr).astype(np.float32)
    cosT = np.tile(np.concatenate([cos32, cos32], axis=0), (2, 1))
    sinS = np.tile(np.concatenate([-sin32, sin32], axis=0), (2, 1))
    p = np.arange(128)[:, None]
    f = np.arange(NG)[None, :]
    masks = np.stack([(f >= 128 * j + p).astype(np.float32) for j in range(4)])
    return (np.ascontiguousarray(cosT).astype(bfdt),
            np.ascontiguousarray(sinS).astype(bfdt),
            np.ascontiguousarray(masks).astype(bfdt),
            np.ones((128, 1), np.float32).astype(bfdt),
            np.ones((1, 128), np.float32).astype(bfdt))


def prep_weights(d):
    c = np.ascontiguousarray
    return dict(
        wdqT=c(np.asarray(d["L0_attn_w_dq"], np.float32).T),
        wuqT=c(np.asarray(d["L0_attn_w_uq"], np.float32).T),
        wdkvT=c(np.asarray(d["L0_attn_w_dkv"], np.float32).T),
        wukT=c(np.asarray(d["L0_attn_w_uk"], np.float32).T),
        wuvT=c(np.asarray(d["L0_attn_w_uv"], np.float32).T),
        wqrT=c(np.asarray(d["L0_attn_w_qr"], np.float32).T),
        wkrT=c(np.asarray(d["L0_attn_w_kr"], np.float32).T),
        woutT=c(np.asarray(d["L0_attn_w_out"], np.float32).T),
        wgT=c(np.asarray(d["L0_ffn_wq_gate"], np.int32).astype(np.uint8).T),
        wuT=c(np.asarray(d["L0_ffn_wq_up"], np.int32).astype(np.uint8).T),
        wdT=c(np.asarray(d["L0_ffn_wq_down"], np.int32).astype(np.uint8).T),
        sg=c(np.asarray(d["L0_ffn_s_gate"], np.float32).reshape(HIDC, 128).T),
        su=c(np.asarray(d["L0_ffn_s_up"], np.float32).reshape(HIDC, 128).T),
        sd=c(np.asarray(d["L0_ffn_s_down"], np.float32).reshape(EC, 128).T),
    )


def make_in_map(core, x_full, w, nseq=NSEQ):
    cosT, sinS, masks, onesc, onesr = _host_constants()
    xs = np.ascontiguousarray(
        np.transpose(x_full[core * NSEQ:core * NSEQ + nseq], (0, 2, 1)))
    return dict(
        x_in=xs,
        wdqT=w["wdqT"], wuqT=w["wuqT"], wdkvT=w["wdkvT"], wukT=w["wukT"],
        wuvT=w["wuvT"], wqrT=w["wqrT"], wkrT=w["wkrT"], woutT=w["woutT"],
        wgT=w["wgT"], wuT=w["wuT"], wdT=w["wdT"],
        sg_in=w["sg"], su_in=w["su"], sd_in=w["sd"],
        cos_in=cosT, sin_in=sinS, mask_in=masks, ones_c=onesc, ones_r=onesr,
    )


def kernel(**inputs):
    x_full = np.asarray(inputs["L0_attn_x"], np.float32)
    w = prep_weights(inputs)
    nc = build_core_program()
    in_maps = [make_in_map(c, x_full, w) for c in range(8)]
    res = run_bass_kernel_spmd(nc, in_maps, core_ids=list(range(8)))
    outs = [np.transpose(res.results[c]["y_out"], (0, 2, 1)) for c in range(8)]
    return np.ascontiguousarray(np.concatenate(outs, axis=0))
